# version 03
# baseline (speedup 1.0000x reference)
import math
import numpy as np
import ml_dtypes
from contextlib import ExitStack

import concourse.bass as bass
import concourse.mybir as mybir
from concourse.bass_utils import run_bass_kernel_spmd

F32 = mybir.dt.float32
BF16 = mybir.dt.bfloat16
U8 = mybir.dt.uint8
AF = mybir.ActivationFunctionType
ALU = mybir.AluOpType

NCORE = 8
TPC = 1024
D = 2048
NCH = D // 128
HID = 8192
NHCH = HID // 128

BF = ml_dtypes.bfloat16


def _hadamard(n):
    h = np.array([[1.0]], dtype=np.float32)
    while h.shape[0] < n:
        h = np.block([[h, h], [h, -h]])
    return h


def host_consts():
    c = {}
    c["H128"] = _hadamard(128).astype(BF)
    c["H64S"] = (_hadamard(64) / math.sqrt(8192.0)).astype(np.float32).astype(BF)
    i = np.arange(128)
    c["TRIU"] = (i[:, None] <= i[None, :]).astype(np.uint8)
    c["POSC"] = np.broadcast_to((i - 64).astype(np.float32)[:, None], (128, 16)).copy()
    ar = np.arange(128)
    rep = np.zeros((8, 64, 128), dtype=BF)
    nrm = np.zeros((8, 128, 64), dtype=BF)
    for kk in range(8):
        for al in range(8):
            rep[kk, kk * 8 + al, al * 16:(al + 1) * 16] = 1
            nrm[kk, al * 16:(al + 1) * 16, kk * 8 + al] = 1
    c["IND_REP"] = rep
    c["IND_NRM"] = nrm
    selr = np.zeros((128, 16), dtype=BF)
    vrep = np.zeros((16, 128), dtype=BF)
    for al in range(8):
        for r in range(16):
            selr[al * 16 + r, r] = 1
            vrep[r, al * 16 + r] = 1
    c["IND_SELR"] = selr
    c["IND_VREP"] = vrep
    j = np.arange(64)
    su = (j[:, None] < j[None, :]).astype(BF)
    iu = (j[:, None] <= j[None, :]).astype(BF)
    sl = (j[None, :] < j[:, None]).astype(BF)
    eye = np.eye(64, dtype=BF)
    c["MASKS"] = np.concatenate([su, iu, sl, eye], axis=1)
    c["MASKS2"] = np.concatenate([su, iu, su, iu, sl], axis=1)
    ind2 = np.zeros((128, 2), dtype=BF)
    ind2[0:64, 0] = 1
    ind2[64:128, 1] = 1
    c["IND2H"] = ind2
    c["IND2HT"] = ind2.T.copy()
    return c


class Sem:
    def __init__(self, handle):
        self.h = handle
        self.n = 0


class K:
    ENGS = ("tensor", "vector", "scalar", "gpsimd", "sync")

    def __init__(self, races=True):
        self.nc = bass.Bass("TRN2", target_bir_lowering=False,
                            detect_race_conditions=races)
        self.es = ExitStack()
        self.streams = {e: [] for e in self.ENGS}
        self._semcnt = 0
        self._sbcnt = 0
        self.io_map = {}
        self.fused = False
        self._seg_es = None
        self._dram_mark = None

    def seg_begin(self, io=None):
        self.io_map = io or {}
        self._seg_es = ExitStack()
        self.streams = {e: [] for e in self.ENGS}

    def seg_run(self):
        streams = self.streams
        with self.nc.Block() as block:
            @block.tensor
            def _(e):
                for f in streams["tensor"]:
                    f(e)

            @block.vector
            def _(e):
                for f in streams["vector"]:
                    f(e)

            @block.scalar
            def _(e):
                for f in streams["scalar"]:
                    f(e)

            @block.gpsimd
            def _(e):
                for f in streams["gpsimd"]:
                    f(e)

            @block.sync
            def _(e):
                for f in streams["sync"]:
                    f(e)
        self._seg_es.close()
        self._seg_es = None
        self.streams = {e: [] for e in self.ENGS}
        self.nc.reset(
            previous_local_dram_base=self._dram_mark[0],
            previous_shared_dram_base=self._dram_mark[1],
        )

    def mark_dram(self):
        self._dram_mark = (self.nc.local_dram_base, self.nc.shared_dram_base)

    def _ctx(self):
        return self._seg_es if self._seg_es is not None else self.es

    def sem(self):
        self._semcnt += 1
        return Sem(self._ctx().enter_context(self.nc.semaphore(f"s{self._semcnt}")))

    def sb(self, name, shape, dt=F32):
        self._sbcnt += 1
        return self._ctx().enter_context(
            self.nc.sbuf_tensor(f"{name}_{self._sbcnt}", list(shape), dt))

    def ps(self, name, shape=(128, 512)):
        self._sbcnt += 1
        return self._ctx().enter_context(
            self.nc.psum_tensor(f"{name}_{self._sbcnt}", list(shape), F32))

    def din(self, name, shape, dt=F32):
        if name in self.io_map:
            return self.io_map[name]
        return self.nc.dram_tensor(name, list(shape), dt, kind="ExternalInput").ap()

    def dout(self, name, shape, dt=F32):
        if name in self.io_map:
            return self.io_map[name]
        return self.nc.dram_tensor(name, list(shape), dt, kind="ExternalOutput").ap()

    def dint(self, name, shape, dt=F32, **kw):
        self._sbcnt += 1
        return self.nc.dram_tensor(f"{name}_{self._sbcnt}", list(shape), dt,
                                   kind="Internal", **kw).ap()

    def wait(self, eng, sem, val):
        self.streams[eng].append(lambda e, s=sem.h, v=val: e.wait_ge(s, v))

    def raw(self, eng, fn):
        self.streams[eng].append(fn)

    def fence(self, sems):
        for eng in self.ENGS:
            for sem in sems:
                if sem.n > 0:
                    self.wait(eng, sem, sem.n)

    def op(self, eng, meth, *a, inc=None, inc16=None, **kw):
        incs = []
        if inc is not None:
            for s in (inc if isinstance(inc, list) else [inc]):
                s.n += 1
                incs.append((s.h, 1))
        if inc16 is not None:
            for s in (inc16 if isinstance(inc16, list) else [inc16]):
                if s.n > 0:
                    self.wait(eng, s, s.n)
                s.n += 16
                incs.append((s.h, 16))

        def fn(e, meth=meth, a=a, kw=kw, incs=incs):
            inst = getattr(e, meth)(*a, **kw)
            for h, v in incs:
                inst.then_inc(h, v)
        self.streams[eng].append(fn)

    def simulate(self, in_map, outnames=None):
        from concourse import bass_interp
        with self.es:
            with self.nc.Block() as block:
                streams = self.streams

                @block.tensor
                def _(e):
                    for f in streams["tensor"]:
                        f(e)

                @block.vector
                def _(e):
                    for f in streams["vector"]:
                        f(e)

                @block.scalar
                def _(e):
                    for f in streams["scalar"]:
                        f(e)

                @block.gpsimd
                def _(e):
                    for f in streams["gpsimd"]:
                        f(e)

                @block.sync
                def _(e):
                    for f in streams["sync"]:
                        f(e)
            sim = bass_interp.MultiCoreSim(self.nc, 1)
            for kname, v in in_map.items():
                sim.cores[0].tensor(kname)[:] = v
            sim.simulate()
            outs = {}
            for nm in (outnames or []):
                outs[nm] = np.asarray(sim.cores[0].tensor(nm)).copy()
            return outs

    def launch(self, in_map_list, **kw):
        self.es.close()
        return run_bass_kernel_spmd(
            self.nc, in_map_list, core_ids=list(range(len(in_map_list))), **kw)

    def run(self, in_map_list, **kw):
        with self.es:
            with self.nc.Block() as block:
                streams = self.streams

                @block.tensor
                def _(e):
                    for f in streams["tensor"]:
                        f(e)

                @block.vector
                def _(e):
                    for f in streams["vector"]:
                        f(e)

                @block.scalar
                def _(e):
                    for f in streams["scalar"]:
                        f(e)

                @block.gpsimd
                def _(e):
                    for f in streams["gpsimd"]:
                        f(e)

                @block.sync
                def _(e):
                    for f in streams["sync"]:
                        f(e)
            return run_bass_kernel_spmd(
                self.nc, in_map_list, core_ids=list(range(len(in_map_list))), **kw)


def emit_wtrans(k, src_ap, dst_ap, nslab, ncol, nat, tp, s_ld, s_tp, s_st):
    b_ld, b_tp, b_st = s_ld.n, s_tp.n, s_st.n
    for s in range(nslab):
        b = s % 2
        k.wait("gpsimd", s_tp, b_tp + 16 * (s - 1) if s >= 2 else b_tp)
        k.op("gpsimd", "dma_start", out=nat[b][:, 0:ncol * 128],
             in_=src_ap[s * 128:(s + 1) * 128, :], inc16=s_ld)
        k.wait("sync", s_ld, b_ld + 16 * (s + 1))
        k.wait("sync", s_st, b_st + 16 * (s - 1) if s >= 2 else b_st)
        k.op("sync", "dma_start", out=tp[b][:, 0:ncol, :], in_=nat[b][:, 0:ncol * 128],
             transpose=True, inc16=s_tp)
        k.wait("scalar", s_tp, b_tp + 16 * (s + 1))
        k.op("scalar", "dma_start", out=dst_ap[s], in_=tp[b][:, 0:ncol, :], inc16=s_st)


def build_wprep(k=None):
    if k is None:
        k = K()
    W1 = k.din("W1", [HID, D])
    W2 = k.din("W2", [D, HID])
    W3 = k.din("W3", [HID, D])
    A = k.din("A", [64, 16, D])
    B = k.din("B", [64, D, 16])
    Wf = k.din("Wf", [D, D])
    H64S = k.din("H64S", [64, 64], BF16)

    w1t = k.dout("w1t", [NHCH, 128, NCH, 128], BF16)
    w3t = k.dout("w3t", [NHCH, 128, NCH, 128], BF16)
    w2ft = k.dout("w2ft", [NHCH, 128, D], BF16)
    a_t = k.dout("a_t", [8, 128, NCH, 128], BF16)
    w_t = k.dout("w_t", [NCH, 128, NCH, 128], BF16)
    b_t = k.dout("b_t", [8, 128, NCH, 128], BF16)

    w2tmp = k.dint("w2tmp", [NHCH, 128, D], BF16)

    nat = [k.sb(f"nat{i}", [128, D], BF16) for i in range(2)]
    tp = [k.sb(f"tp{i}", [128, NCH, 128], BF16) for i in range(2)]
    s_ld, s_tp, s_st = k.sem(), k.sem(), k.sem()

    emit_wtrans(k, W1, w1t, NHCH, NCH, nat, tp, s_ld, s_tp, s_st)
    emit_wtrans(k, W3, w3t, NHCH, NCH, nat, tp, s_ld, s_tp, s_st)
    emit_wtrans(k, A.rearrange("a r h -> (a r) h"), a_t, 8, NCH, nat, tp, s_ld, s_tp, s_st)
    emit_wtrans(k, Wf, w_t, NCH, NCH, nat, tp, s_ld, s_tp, s_st)

    bnat = [k.sb(f"bnat{i}", [128, 128], BF16) for i in range(2)]
    btp = k.sb("btp", [128, NCH, 128], BF16)
    sb_ld, sb_tp, sb_st = k.sem(), k.sem(), k.sem()
    for g in range(8):
        for j in range(NCH):
            it = g * NCH + j
            b = it % 2
            if it >= 2:
                k.wait("gpsimd", sb_tp, 16 * (it - 1))
            src = B[g * 8:(g + 1) * 8, j * 128:(j + 1) * 128, :].rearrange("a p r -> p a r")
            k.op("gpsimd", "dma_start",
                 out=bnat[b][:].rearrange("p (a r) -> p a r", a=8), in_=src, inc16=sb_ld)
            k.wait("sync", sb_ld, 16 * (it + 1))
            if j == 0 and g > 0:
                k.wait("sync", sb_st, 16 * g)
            k.op("sync", "dma_start", out=btp[:, j, :], in_=bnat[b][:], transpose=True,
                 inc16=sb_tp)
        k.wait("scalar", sb_tp, 16 * (g + 1) * NCH)
        k.op("scalar", "dma_start", out=b_t[g], in_=btp[:], inc16=sb_st)

    w2nat = [k.sb(f"w2nat{i}", [128, HID], BF16) for i in range(2)]
    w2tp = [k.sb(f"w2tp{i}", [128, NHCH, 128], BF16) for i in range(2)]
    s2_ld, s2_tp, s2_st = k.sem(), k.sem(), k.sem()
    for i in range(NCH):
        b = i % 2
        if i >= 2:
            k.wait("gpsimd", s2_tp, 16 * (i - 1))
        k.op("gpsimd", "dma_start", out=w2nat[b][:], in_=W2[i * 128:(i + 1) * 128, :],
             inc16=s2_ld)
        k.wait("sync", s2_ld, 16 * (i + 1))
        if i >= 2:
            k.wait("sync", s2_st, 16 * (i - 1))
        k.op("sync", "dma_start", out=w2tp[b][:], in_=w2nat[b][:], transpose=True,
             inc16=s2_tp)
        k.wait("scalar", s2_tp, 16 * (i + 1))
        k.op("scalar", "dma_start",
             out=w2tmp[:, :, i * 128:(i + 1) * 128].rearrange("m p d -> p m d"),
             in_=w2tp[b][:], inc16=s2_st)

    h64_sb = k.sb("h64_sb", [64, 64], BF16)
    alt = [k.sb(f"alt{i}", [64, D], BF16) for i in range(2)]
    fold = [k.sb(f"fold{i}", [64, D], BF16) for i in range(2)]
    pbf = [k.ps(f"pbf{i}", (64, 512)) for i in range(4)]
    sf_ld, sf_mm, sf_ev, sf_st = k.sem(), k.sem(), k.sem(), k.sem()
    k.op("sync", "dma_start", out=h64_sb[:], in_=H64S, inc16=sf_ld)
    k.wait("gpsimd", s2_st, 16 * NCH)
    for p in range(128):
        b = p % 2
        if p >= 2:
            k.wait("gpsimd", sf_mm, p - 1)
        k.op("gpsimd", "dma_start", out=alt[b][:], in_=w2tmp[:, p, :], inc16=sf_ld)
        k.wait("tensor", sf_ld, 16 * (p + 2))
        if p >= 1:
            k.wait("tensor", sf_ev, p)
        for j in range(4):
            k.op("tensor", "matmul", pbf[j][:, :], h64_sb[:],
                 alt[b][:, j * 512:(j + 1) * 512], start=True, stop=True,
                 inc=(sf_mm if j == 3 else None))
        k.wait("vector", sf_mm, p + 1)
        if p >= 2:
            k.wait("vector", sf_st, 16 * (p - 1))
        for j in range(4):
            k.op("vector", "tensor_copy", fold[b][:, j * 512:(j + 1) * 512], pbf[j][:, :],
                 inc=(sf_ev if j == 3 else None))
        k.wait("scalar", sf_ev, p + 1)
        k.op("scalar", "dma_start", out=w2ft[:, p, :], in_=fold[b][:], inc16=sf_st)

    k.wait("sync", s_st, s_st.n)
    k.wait("sync", sb_st, sb_st.n)
    k.wait("sync", sf_st, sf_st.n)
    return k


def emit_ln(k, r_src, rres, xln, bufs, t0_tile, ntile, s_done, gate=None, nbuf=None):
    bnst, stats, xlt, epscol = bufs
    s_ld, s_bn, s_rs, s_ap = k.sem(), k.sem(), k.sem(), k.sem()
    b_bn, b_rs, b_ap, b_done = s_bn.n, s_rs.n, s_ap.n, s_done.n
    if nbuf is None:
        nbuf = ntile
    if gate is not None:
        k.wait("sync", gate[0], gate[1])
    for tau in range(ntile):
        tt = t0_tile + tau
        if tau >= nbuf:
            k.wait("sync", s_ap, b_ap + tau - nbuf + 1)
        k.op("sync", "dma_start", out=rres[:, tau % nbuf, :],
             in_=r_src[tt * 128:(tt + 1) * 128, :], inc16=s_ld)
        k.wait("vector", s_ld, 16 * (tau + 1))
        if gate is not None and tau == 0:
            k.wait("vector", gate[0], gate[1])
        if tau >= 1:
            k.wait("vector", s_rs, b_rs + tau)
        for i in range(4):
            k.op("vector", "bn_stats", bnst[:, i * 6:(i + 1) * 6],
                 rres[:, tau % nbuf, i * 512:(i + 1) * 512])
        k.op("vector", "drain")
        k.op("vector", "bn_aggr", stats[:, 0:2], bnst[:], inc=s_bn)
        k.wait("scalar", s_bn, b_bn + tau + 1)
        k.op("scalar", "activation", stats[:, 2:3], stats[:, 1:2], AF.Ln,
             bias=epscol[:, 0:1])
        k.op("scalar", "drain")
        k.op("scalar", "activation", stats[:, 2:3], stats[:, 2:3], AF.Exp, scale=-0.5,
             inc=s_rs)
        k.wait("vector", s_rs, b_rs + tau + 1)
        k.wait("vector", s_done, s_done.n)
        k.op("vector", "drain")
        k.op("vector", "tensor_scalar", xlt[:], rres[:, tau % nbuf, :], stats[:, 0:1],
             stats[:, 2:3], ALU.subtract, ALU.mult, inc=s_ap)
        k.wait("sync", s_ap, b_ap + tau + 1)
        k.op("sync", "dma_start", out=xln[:, :, tau * 128:(tau + 1) * 128], in_=xlt[:],
             transpose=True, inc16=s_done)


def build_ffn0(k=None):
    if k is None:
        k = K()
    r_in = k.din("r_in", [TPC, D])
    w1t = k.din("w1t", [NHCH, 128, NCH, 128], BF16)
    w3t = k.din("w3t", [NHCH, 128, NCH, 128], BF16)
    w2ft = k.din("w2ft", [NHCH, 128, D], BF16)
    H128 = k.din("H128", [128, 128], BF16)
    r_out = k.dout("r_out", [TPC, D])

    xln = k.sb("xln", [128, NCH, 512], BF16)
    rres = k.sb("rres", [128, 4, D])
    hid = k.sb("hid", [128, NHCH, 512], BF16)
    h128_sb = k.sb("h128_sb", [128, 128], BF16)
    bnst = k.sb("bnst", [128, 24])
    stats = k.sb("stats", [128, 4])
    epscol = k.sb("epscol", [128, 1])
    xlt = k.sb("xlt", [128, D], BF16)
    w1s = [k.sb(f"w1s{i}", [128, NCH, 128], BF16) for i in range(2)]
    w3s = [k.sb(f"w3s{i}", [128, NCH, 128], BF16) for i in range(2)]
    sil = k.sb("sil", [128, 512], BF16)
    w2s = [k.sb(f"w2s{i}", [128, 32, 128], BF16) for i in range(2)]
    obuf = [k.sb(f"obuf{i}", [128, 512], BF16) for i in range(2)]
    otok = k.sb("otok", [128, 4, D], BF16)
    rfin = k.sb("rfin", [128, D])
    pb = [k.ps(f"pb{i}") for i in range(8)]

    s_h = k.sem()
    k.op("sync", "dma_start", out=h128_sb[:], in_=H128, inc16=s_h)
    k.wait("tensor", s_h, 16)
    k.op("vector", "memset", epscol[:], 1e-5)
    k.op("vector", "drain")

    s_res = k.sem()
    s_fin = k.sem()
    for half in range(2):
        t0 = half * 512
        s_tp = k.sem()
        gate = (s_fin, 4 * half) if half > 0 else None
        emit_ln(k, r_in, rres, xln, (bnst, stats, xlt, epscol), half * 4, 4, s_tp,
                gate=gate)
        s_w = k.sem(); s_mm = k.sem(); s_sil = k.sem(); s_hid = k.sem(); s_y1 = k.sem()
        k.wait("tensor", s_tp, 16 * 4)
        for s in range(NHCH):
            b = s % 2
            if s >= 2:
                k.wait("sync", s_mm, 3 * (s - 1))
            k.op("sync", "dma_start", out=w1s[b][:], in_=w1t[s], inc16=s_w)
            k.op("sync", "dma_start", out=w3s[b][:], in_=w3t[s], inc16=s_w)
            k.wait("tensor", s_w, 32 * (s + 1))
            bk = b * 3
            if s >= 2:
                k.wait("tensor", s_hid, s - 1)
                k.wait("tensor", s_y1, s - 1)
            for c in range(NCH):
                k.op("tensor", "matmul", pb[bk][:, :], w1s[b][:, c, :], xln[:, c, :],
                     start=(c == 0), stop=(c == NCH - 1),
                     inc=(s_mm if c == NCH - 1 else None))
            for c in range(NCH):
                k.op("tensor", "matmul", pb[bk + 1][:, :], w3s[b][:, c, :], xln[:, c, :],
                     start=(c == 0), stop=(c == NCH - 1),
                     inc=(s_mm if c == NCH - 1 else None))
            k.wait("scalar", s_mm, 3 * s + 1)
            k.op("scalar", "activation", sil[:], pb[bk][:, :], AF.Silu, inc=s_sil)
            k.wait("vector", s_mm, 3 * s + 2)
            k.wait("vector", s_sil, s + 1)
            k.op("vector", "tensor_mul", hid[:, s, :], sil[:], pb[bk + 1][:, :], inc=s_hid)
            k.wait("tensor", s_hid, s + 1)
            k.op("tensor", "matmul", pb[bk + 2][:, :], h128_sb[:], hid[:, s, :],
                 start=True, stop=True, inc=s_mm)
            k.wait("scalar", s_mm, 3 * (s + 1))
            k.op("scalar", "activation", hid[:, s, :], pb[bk + 2][:, :], AF.Copy, inc=s_y1)
        s_w2 = k.sem(); s_dmm = k.sem(); s_kh = k.sem(); s_ev = k.sem(); s_ot = k.sem()
        k.wait("sync", s_y1, NHCH)
        for n in range(NCH):
            dbk = 6 + (n % 2)
            for mh in range(2):
                it = n * 2 + mh
                b = it % 2
                if n >= 1:
                    if mh == 0:
                        k.wait("sync", s_kh, n)
                    else:
                        k.wait("sync", s_dmm, n)
                k.op("sync", "dma_start", out=w2s[b][:],
                     in_=w2ft[mh * 32:(mh + 1) * 32, :, n * 128:(n + 1) * 128]
                     .rearrange("m p d -> p m d"), inc16=s_w2)
                k.wait("tensor", s_w2, 16 * (it + 1))
                if mh == 0 and n >= 2:
                    k.wait("tensor", s_ev, n - 1)
                for m in range(32):
                    gm = mh * 32 + m
                    k.op("tensor", "matmul", pb[dbk][:, :], w2s[b][:, m, :],
                         hid[:, gm, :], start=(gm == 0), stop=(gm == NHCH - 1),
                         inc=(s_dmm if gm == NHCH - 1 else
                              (s_kh if gm == 31 else None)))
            ob = n % 2
            k.wait("scalar", s_dmm, n + 1)
            if n >= 2:
                k.wait("scalar", s_ot, 16 * (n - 1))
            k.op("scalar", "activation", obuf[ob][:], pb[dbk][:, :], AF.Copy, inc=s_ev)
            k.wait("sync", s_ev, n + 1)
            k.op("sync", "dma_start", out=otok[:, :, n * 128:(n + 1) * 128],
                 in_=obuf[ob][:], transpose=True, inc16=s_ot)
        k.wait("vector", s_ot, 16 * NCH)
        for tau in range(4):
            if half * 4 + tau >= 1:
                k.wait("vector", s_res, 16 * (half * 4 + tau))
            k.op("vector", "drain")
            k.op("vector", "scalar_tensor_tensor", rfin[:], otok[:, tau, :], 0.5,
                 rres[:, tau, :], ALU.mult, ALU.add, inc=s_fin)
            k.wait("scalar", s_fin, half * 4 + tau + 1)
            tt = half * 4 + tau
            k.op("scalar", "dma_start", out=r_out[tt * 128:(tt + 1) * 128, :],
                 in_=rfin[:], inc16=s_res)
    k.wait("sync", s_res, 16 * 8)
    return k




def ref_ffn0(x, W1, W2, W3):
    import jax.numpy as jnp
    import jax
    mu = x.mean(-1, keepdims=True)
    var = x.var(-1, keepdims=True)
    xl = (x - mu) / np.sqrt(var + 1e-5)
    ug = xl @ W1.T
    ul = xl @ W3.T
    h = (ug / (1 + np.exp(-ug))) * ul
    n = h.shape[-1]
    hh = h.reshape(-1, n).copy()
    step = 1
    while step < n:
        hb = hh.reshape(-1, n // (2 * step), 2, step)
        a = hb[:, :, 0, :].copy()
        b = hb[:, :, 1, :].copy()
        hb[:, :, 0, :] = a + b
        hb[:, :, 1, :] = a - b
        step *= 2
    hh = (hh / math.sqrt(n)).reshape(h.shape)
    return x + 0.5 * (hh @ W2.T)


def build_qkv0(k=None):
    if k is None:
        k = K()
    r_in = k.din("r_in", [TPC, D])
    Wqkv = k.din("Wqkv", [D, 3 * D])
    bqkv = k.din("bqkv", [3 * D])
    slopes = k.din("slopes", [16])
    POSC = k.din("POSC", [128, 16])
    qd_fm = k.dout("qd_fm", [128, NCH, TPC], BF16)
    ku_fm = k.dout("ku_fm", [128, NCH, TPC], BF16)
    ku_tok = k.dout("ku_tok", [TPC, 16, 128], BF16)
    v_tok = k.dout("v_tok", [TPC, 16, 128], BF16)

    xln = k.sb("xln", [128, NCH, TPC], BF16)
    rres = k.sb("rres", [128, 8, D])
    bnst = k.sb("bnst", [128, 24])
    stats = k.sb("stats", [128, 4])
    epscol = k.sb("epscol", [128, 1])
    xlt = k.sb("xlt", [128, D], BF16)
    slp = k.sb("slp", [1, 16])
    onesf = k.sb("onesf", [1, 128])
    posc_sb = k.sb("posc_sb", [128, 16])
    srep = k.sb("srep", [128, 16])
    u16 = k.sb("u16", [128, 16])
    w16 = k.sb("w16", [128, 16])
    wq = [k.sb(f"wq{i}", [128, 512], BF16) for i in range(2)]
    bq = k.sb("bq", [1, 3 * D], BF16)
    ones1 = k.sb("ones1", [1, 128], BF16)
    ev = [k.sb(f"ev{i}", [128, 512], BF16) for i in range(2)]
    tst = [k.sb(f"tst{i}", [128, 4, 128], BF16) for i in range(2)]
    pb = [k.ps(f"pb{i}") for i in range(8)]
    pdec = pb[0]

    s_c = k.sem()
    k.op("sync", "dma_start", out=slp[:], in_=slopes[None, :], inc16=s_c)
    k.op("sync", "dma_start", out=posc_sb[:], in_=POSC, inc16=s_c)
    k.op("gpsimd", "dma_start", out=bq[:], in_=bqkv[None, :], inc16=s_c)
    k.op("vector", "memset", epscol[:], 1e-5)
    k.op("vector", "memset", onesf[:], 1.0)
    k.op("vector", "memset", ones1[:], 1.0)
    k.op("vector", "drain", inc=s_c)
    s_d = k.sem()
    k.wait("tensor", s_c, 49)
    k.op("tensor", "matmul", pdec[:, 0:16], onesf[:], slp[:], start=True, stop=True, inc=s_d)
    k.wait("vector", s_d, 1)
    k.op("vector", "tensor_mul", srep[:], posc_sb[:], pdec[:, 0:16], inc=s_d)
    k.wait("scalar", s_d, 2)
    k.op("scalar", "activation", u16[:], srep[:], AF.Exp, scale=-1.0)
    k.op("scalar", "activation", w16[:], srep[:], AF.Exp, scale=1.0, inc=s_d)

    s_tp = k.sem()
    emit_ln(k, r_in, rres, xln, (bnst, stats, xlt, epscol), 0, 8, s_tp)

    s_w = k.sem(); s_mm = k.sem(); s_ev = k.sem(); s_st = k.sem()
    k.wait("tensor", s_tp, 16 * 8)
    k.wait("tensor", s_d, 3)
    k.wait("vector", s_d, 3)
    wq_done = {}
    ev_store = {}
    s_ev16 = k.sem()
    nev = 0
    for o in range(12):
        for c in range(NCH):
            it = o * NCH + c
            b = it % 2
            if it >= 2:
                k.wait("gpsimd", s_mm, wq_done[it - 2])
            k.op("gpsimd", "dma_start", out=wq[b][:],
                 in_=Wqkv[c * 128:(c + 1) * 128, o * 512:(o + 1) * 512], inc16=s_w)
            k.wait("tensor", s_w, 16 * (it + 1))
            if c == 0 and o >= 1:
                k.wait("tensor", s_ev, o * 8)
            for tau in range(8):
                k.op("tensor", "matmul", pb[tau][:, :], xln[:, c, tau * 128:(tau + 1) * 128],
                     wq[b][:], start=(c == 0), stop=False,
                     inc=(s_mm if tau == 7 else None))
            wq_done[it] = s_mm.n
        for tau in range(8):
            k.op("tensor", "matmul", pb[tau][:, :], ones1[:, :],
                 bq[0:1, o * 512:(o + 1) * 512], start=False, stop=True,
                 inc=(s_mm if tau == 7 else None))
        mm_all = s_mm.n
        kind = "q" if o < 4 else ("k" if o < 8 else "v")
        for tau in range(8):
            eb = nev % 2
            k.wait("vector", s_mm, mm_all)
            if nev >= 2:
                k.wait("vector", s_st, ev_store[nev - 2])
            if kind == "v":
                k.op("vector", "tensor_copy", ev[eb][:], pb[tau][:, :], inc=s_ev)
            else:
                dtab = u16 if kind == "q" else w16
                for h in range(4):
                    hh = (o % 4) * 4 + h
                    k.op("vector", "tensor_scalar", ev[eb][:, h * 128:(h + 1) * 128],
                         pb[tau][:, h * 128:(h + 1) * 128], dtab[:, hh:hh + 1], None,
                         ALU.mult, inc=(s_ev if h == 3 else None))
            k.wait("sync", s_ev, s_ev.n)
            k.wait("scalar", s_ev, s_ev.n)
            if kind in ("q", "k"):
                dst = qd_fm if kind == "q" else ku_fm
                if nev >= 4:
                    k.wait("sync", s_st, ev_store[nev - 4])
                k.op("sync", "dma_start", out=tst[eb][:], in_=ev[eb][:],
                     transpose=True, inc16=s_ev16)
                k.wait("scalar", s_ev16, s_ev16.n)
                k.op("scalar", "dma_start",
                     out=dst[:, (o % 4) * 4:(o % 4) * 4 + 4, tau * 128:(tau + 1) * 128],
                     in_=tst[eb][:], inc16=s_st)
                if kind == "k":
                    k.op("scalar", "dma_start",
                         out=ku_tok[tau * 128:(tau + 1) * 128,
                                    (o - 4) * 4:(o - 4) * 4 + 4, :],
                         in_=ev[eb][:].rearrange("p (h d) -> p h d", d=128),
                         inc16=s_st)
            else:
                k.op("scalar", "dma_start",
                     out=v_tok[tau * 128:(tau + 1) * 128,
                               (o - 8) * 4:(o - 8) * 4 + 4, :],
                     in_=ev[eb][:].rearrange("p (h d) -> p h d", d=128),
                     inc16=s_st)
            ev_store[nev] = s_st.n
            nev += 1
    k.wait("sync", s_st, s_st.n)
    return k


def build_lscan(k=None):
    if k is None:
        k = K()
    SEQ = 4096
    NC_SEQ = SEQ // 128
    NJ = SEQ // 1024
    qd = k.din("qd", [4, NJ, 128, 1024], BF16)
    kf = k.din("kf", [4, NJ, 128, 1024], BF16)
    kt = k.din("kt", [4, NJ, 1024, 128], BF16)
    vt = k.din("vt", [4, NJ, 1024, 128], BF16)
    slp4 = k.din("slp4", [4])
    TRIU = k.din("TRIU", [128, 128], U8)
    o_fm = k.dout("o_fm", [4, NJ, 128, 1024], BF16)

    qd_s = [k.sb(f"qd_s{i}", [128, SEQ], BF16) for i in range(2)]
    kf_s = [k.sb(f"kf_s{i}", [128, SEQ], BF16) for i in range(2)]
    kt_s = [k.sb(f"kt_s{i}", [128, NC_SEQ, 128], BF16) for i in range(2)]
    vt_s = [k.sb(f"vt_s{i}", [128, NC_SEQ, 128], BF16) for i in range(2)]
    o_s = k.sb("o_s", [128, 4, SEQ], BF16)
    triu_s = k.sb("triu_s", [128, 128], U8)
    zero_s = k.sb("zero_s", [128, 128], BF16)
    sm = [k.sb(f"sm{i}", [128, 128], BF16) for i in range(2)]
    sef = [k.sb(f"sef{i}", [128, 128], BF16) for i in range(2)]
    P = [k.sb(f"P{i}", [128, 128]) for i in range(2)]
    slp_sb = k.sb("slp_sb", [1, 4])
    onesf = k.sb("onesf", [1, 128])
    w128 = k.sb("w128", [128, 4])
    pb = [k.ps(f"pb{i}") for i in range(8)]

    s_c = k.sem()
    k.op("sync", "dma_start", out=slp_sb[:], in_=slp4[None, :], inc16=s_c)
    k.op("sync", "dma_start", out=triu_s[:], in_=TRIU, inc16=s_c)
    k.op("vector", "memset", zero_s[:], 0.0)
    k.op("vector", "memset", onesf[:], 1.0)
    k.op("vector", "drain", inc=s_c)
    s_d = k.sem()
    k.wait("tensor", s_c, 33)
    k.op("tensor", "matmul", pb[6][:, 0:4], onesf[:], slp_sb[:], start=True, stop=True,
         inc=s_d)
    k.wait("scalar", s_d, 1)
    k.op("scalar", "activation", w128[:], pb[6][:, 0:4], AF.Exp, scale=-128.0, inc=s_d)

    s_ld = k.sem(); s_mm1 = k.sem(); s_mm2 = k.sem(); s_mm4 = k.sem()
    s_sel = k.sem(); s_up = k.sem(); s_ev = k.sem(); s_cast = k.sem(); s_st = k.sem()
    k.wait("vector", s_d, 2)
    k.wait("tensor", s_d, 2)
    for i in range(4):
        bi = i % 2
        if i >= 2:
            k.wait("sync", s_mm4, (i - 1) * NC_SEQ)
        for (dst, src) in ((qd_s, qd), (kf_s, kf)):
            k.op("sync", "dma_start",
                 out=dst[bi][:].rearrange("p (j t) -> p j t", t=1024),
                 in_=src[i].rearrange("j p t -> p j t"), inc16=s_ld)
        k.op("sync", "dma_start",
             out=kt_s[bi][:].rearrange("p (j c) d -> p j c d", c=8),
             in_=kt[i].rearrange("j (c p) d -> p j c d", p=128), inc16=s_ld)
        k.op("sync", "dma_start",
             out=vt_s[bi][:].rearrange("p (j c) d -> p j c d", c=8),
             in_=vt[i].rearrange("j (c p) d -> p j c d", p=128), inc16=s_ld)
        k.wait("tensor", s_ld, 16 * 4 * (i + 1))
        for c in range(NC_SEQ):
            pc = c % 2
            csl = slice(c * 128, (c + 1) * 128)
            gc = i * NC_SEQ + c
            b0, b1, b2 = 3 * pc, 3 * pc + 1, 3 * pc + 2
            if gc >= 2:
                k.wait("tensor", s_sel, gc - 1)
            k.op("tensor", "matmul", pb[b0][:, 0:128], kf_s[bi][:, csl],
                 qd_s[bi][:, csl], start=True, stop=True, inc=s_mm1)
            k.wait("vector", s_mm1, gc + 1)
            if gc >= 2:
                k.wait("vector", s_mm2, gc - 1)
            k.op("vector", "select", sm[pc][:], triu_s[:], pb[b0][:, 0:128], zero_s[:],
                 inc=s_sel)
            if c > 0:
                k.wait("vector", s_up, gc)
                k.op("vector", "drain")
                k.op("vector", "tensor_scalar", sef[pc][:], P[(c + 1) % 2][:],
                     w128[:, i:i + 1], None, ALU.mult, inc=s_cast)
            k.wait("tensor", s_sel, s_sel.n)
            if gc >= 2:
                k.wait("tensor", s_ev, gc - 1)
            if c > 0:
                k.wait("tensor", s_cast, s_cast.n)
                k.op("tensor", "matmul", pb[b1][:, 0:128], vt_s[bi][:, c, :], sm[pc][:],
                     start=True, stop=False)
                k.op("tensor", "matmul", pb[b1][:, 0:128], sef[pc][:], qd_s[bi][:, csl],
                     start=False, stop=True, inc=s_mm2)
            else:
                k.op("tensor", "matmul", pb[b1][:, 0:128], vt_s[bi][:, c, :], sm[pc][:],
                     start=True, stop=True, inc=s_mm2)
            if gc >= 2:
                k.wait("tensor", s_up, gc - 1)
            k.op("tensor", "matmul", pb[b2][:, 0:128], kt_s[bi][:, c, :], vt_s[bi][:, c, :],
                 start=True, stop=True, inc=s_mm4)
            k.wait("vector", s_mm4, gc + 1)
            if c == 0:
                k.op("vector", "drain")
                k.op("vector", "tensor_copy", P[0][:], pb[b2][:, 0:128], inc=s_up)
            else:
                k.op("vector", "drain")
                k.op("vector", "scalar_tensor_tensor", P[c % 2][:], P[(c + 1) % 2][:],
                     w128[:, i:i + 1], pb[b2][:, 0:128], ALU.mult, ALU.add, inc=s_up)
            k.wait("scalar", s_mm2, gc + 1)
            k.op("scalar", "activation", o_s[:, i, csl], pb[b1][:, 0:128], AF.Copy,
                 inc=s_ev)
        k.wait("scalar", s_ev, s_ev.n)
        k.op("scalar", "dma_start", out=o_fm[i].rearrange("j p t -> p j t"),
             in_=o_s[:, i, :].rearrange("p (j t) -> p j t", t=1024), inc16=s_st)
    k.wait("sync", s_st, 16 * 4)
    return k


def build_aout0(k=None):
    if k is None:
        k = K()
    r_in = k.din("r_in", [TPC, D])
    o_in = k.din("o_in", [128, NCH, TPC], BF16)
    Wout = k.din("Wout", [D, D])
    bout = k.din("bout", [D])
    r_out = k.dout("r_out", [TPC, D])

    wo = k.sb("wo", [128, NCH, D], BF16)
    ofm = k.sb("ofm", [128, NCH, TPC], BF16)
    bo = k.sb("bo", [128, NCH])
    obuf = [k.sb(f"obuf{i}", [128, 512], BF16) for i in range(2)]
    otok = k.sb("otok", [128, 8, D], BF16)
    rres = [k.sb(f"rres{i}", [128, D]) for i in range(2)]
    rfin = k.sb("rfin", [128, D])
    pb = [k.ps(f"pb{i}") for i in range(8)]

    s_ld = k.sem()
    for e in range(NCH):
        k.op("gpsimd", "dma_start", out=wo[:, e, :], in_=Wout[e * 128:(e + 1) * 128, :],
             inc16=s_ld)
    k.op("sync", "dma_start", out=ofm[:], in_=o_in, inc16=s_ld)
    k.op("sync", "dma_start", out=bo[:],
         in_=bout.rearrange("(n p) -> p n", p=128), allow_slow_non_contiguous=True,
         inc16=s_ld)
    k.wait("tensor", s_ld, 16 * (NCH + 2))

    s_mm = k.sem(); s_ev = k.sem(); s_ot = k.sem(); s_fin = k.sem(); s_res = k.sem()
    nit = 0
    for n in range(NCH):
        for t2 in range(2):
            dbk = nit % 8
            if nit >= 8:
                k.wait("tensor", s_ev, nit - 7)
            for e in range(NCH):
                k.op("tensor", "matmul", pb[dbk][:, :], wo[:, e, n * 128:(n + 1) * 128],
                     ofm[:, e, t2 * 512:(t2 + 1) * 512],
                     start=(e == 0), stop=(e == NCH - 1),
                     inc=(s_mm if e == NCH - 1 else None))
            eb = nit % 2
            k.wait("vector", s_mm, nit + 1)
            if nit >= 2:
                k.wait("vector", s_ot, 16 * (nit - 1))
            k.op("vector", "tensor_scalar", obuf[eb][:], pb[dbk][:, :], bo[:, n:n + 1],
                 None, ALU.add, inc=s_ev)
            k.wait("sync", s_ev, s_ev.n)
            k.op("sync", "dma_start",
                 out=otok[:, t2 * 4:(t2 + 1) * 4, n * 128:(n + 1) * 128],
                 in_=obuf[eb][:], transpose=True, inc16=s_ot)
            nit += 1
    for tau in range(8):
        b = tau % 2
        if tau >= 2:
            k.wait("scalar", s_fin, tau - 1)
        k.op("scalar", "dma_start", out=rres[b][:],
             in_=r_in[tau * 128:(tau + 1) * 128, :], inc16=s_ld)
        k.wait("vector", s_ot, 16 * 32)
        k.wait("vector", s_ld, 16 * (NCH + 2 + tau + 1))
        if tau >= 1:
            k.wait("vector", s_res, 16 * tau)
        k.op("vector", "drain")
        k.op("vector", "tensor_add", rfin[:], otok[:, tau, :], rres[b][:], inc=s_fin)
        k.wait("sync", s_fin, tau + 1)
        k.op("sync", "dma_start", out=r_out[tau * 128:(tau + 1) * 128, :], in_=rfin[:],
             inc16=s_res)
    k.wait("sync", s_res, 16 * 8)
    return k


def build_ffn1(k=None):
    if k is None:
        k = K()
    r_in = k.din("r_in", [TPC, D])
    a_t = k.din("a_t", [8, 128, NCH, 128], BF16)
    b_t = k.din("b_t", [8, 128, NCH, 128], BF16)
    w_t = k.din("w_t", [NCH, 128, NCH, 128], BF16)
    bias = k.din("bias", [D])
    IND_REP = k.din("IND_REP", [8, 64, 128], BF16)
    IND_NRM = k.din("IND_NRM", [8, 128, 64], BF16)
    IND_SELR = k.din("IND_SELR", [128, 16], BF16)
    IND_VREP = k.din("IND_VREP", [16, 128], BF16)
    r_out = k.dout("r_out", [TPC, D])

    xln = k.sb("xln", [128, NCH, TPC], BF16)
    rres = k.sb("rres", [128, 2, D])
    bnst = k.sb("bnst", [128, 24])
    stats = k.sb("stats", [128, 4])
    epscol = k.sb("epscol", [128, 1])
    xlt = k.sb("xlt", [128, D], BF16)
    vecs = k.sb("vecs", [128, 8, TPC], BF16)
    sqv = k.sb("sqv", [128, 512], BF16)
    arr = k.sb("arr", [64, TPC], BF16)
    arr_tok = k.sb("arr_tok", [128, 8, 64], BF16)
    mx8 = k.sb("mx8", [128, 8])
    routes_tok = k.sb("routes_tok", [128, 8, 128], BF16)
    routes_fm = k.sb("routes_fm", [128, TPC], BF16)
    irep = k.sb("irep", [64, 8, 128], BF16)
    inrm = k.sb("inrm", [128, 8, 64], BF16)
    iselr = k.sb("iselr", [128, 16], BF16)
    ivrep = k.sb("ivrep", [16, 128], BF16)
    tmp = k.sb("tmp", [128, 512], BF16)
    vfm = k.sb("vfm", [16, TPC], BF16)
    vrep_sb = k.sb("vrep_sb", [128, TPC], BF16)
    h2 = k.sb("h2", [128, 8, TPC], BF16)
    ats = [k.sb(f"ats{i}", [128, NCH, 128], BF16) for i in range(2)]
    wts = [k.sb(f"wts{i}", [128, NCH, 128], BF16) for i in range(2)]
    bts = k.sb("bts", [128, 8, NCH, 128], BF16)
    bcol = k.sb("bcol", [128, NCH])
    obuf = [k.sb(f"obuf{i}", [128, 512], BF16) for i in range(2)]
    otok = k.sb("otok", [128, 8, D], BF16)
    rfin = k.sb("rfin", [128, D])
    rrl = [k.sb(f"rrl{i}", [128, D]) for i in range(2)]
    pb = [k.ps(f"pb{i}") for i in range(8)]

    s_c = k.sem()
    k.op("sync", "dma_start", out=irep[:], in_=IND_REP.rearrange("g a f -> a g f"),
         inc16=s_c)
    k.op("sync", "dma_start", out=inrm[:], in_=IND_NRM.rearrange("g p f -> p g f"),
         inc16=s_c)
    k.op("sync", "dma_start", out=iselr[:], in_=IND_SELR, inc16=s_c)
    k.op("sync", "dma_start", out=ivrep[:], in_=IND_VREP, inc16=s_c)
    k.op("sync", "dma_start", out=bts[:], in_=b_t.rearrange("g p k f -> p g k f"),
         inc16=s_c)
    k.op("sync", "dma_start", out=bcol[:], in_=bias.rearrange("(n p) -> p n", p=128),
         allow_slow_non_contiguous=True, inc16=s_c)
    k.op("vector", "memset", epscol[:], 1e-5)
    k.op("vector", "drain", inc=s_c)

    s_tp = k.sem()
    emit_ln(k, r_in, rres, xln, (bnst, stats, xlt, epscol), 0, 8, s_tp, nbuf=2)

    s_w = k.sem(); s_mm = k.sem(); s_ev = k.sem(); s_sq = k.sem(); s_nrm = k.sem()
    k.wait("tensor", s_tp, 16 * 8)
    k.wait("tensor", s_c, s_c.n)
    for kk in range(8):
        b = kk % 2
        if kk >= 2:
            k.wait("sync", s_mm, (kk - 1) * 2)
        k.op("sync", "dma_start", out=ats[b][:], in_=a_t[kk], inc16=s_w)
        k.wait("tensor", s_w, 16 * (kk + 1))
        for t2 in range(2):
            vbk = 2 * b + t2
            if kk >= 2:
                k.wait("tensor", s_sq, (kk - 1) * 2)
            for c in range(NCH):
                k.op("tensor", "matmul", pb[vbk][:, :], ats[b][:, c, :],
                     xln[:, c, t2 * 512:(t2 + 1) * 512], start=(c == 0),
                     stop=(c == NCH - 1), inc=(s_mm if c == NCH - 1 else None))
            k.wait("vector", s_mm, kk * 2 + t2 + 1)
            k.op("vector", "tensor_copy", vecs[:, kk, t2 * 512:(t2 + 1) * 512],
                 pb[vbk][:, :], inc=s_ev)
            k.wait("scalar", s_ev, s_ev.n)
            if kk > 0 or t2 > 0:
                k.wait("scalar", s_nrm, s_nrm.n)
            k.op("scalar", "activation", sqv[:], vecs[:, kk, t2 * 512:(t2 + 1) * 512],
                 AF.Square, inc=s_sq)
            k.wait("tensor", s_sq, kk * 2 + t2 + 1)
            k.op("tensor", "matmul", pb[4 + t2][0:64, :], inrm[:, kk, :], sqv[:],
                 start=(kk == 0), stop=(kk == 7), inc=s_nrm)
    s_ar = k.sem(); s_tk = k.sem(); s_rt = k.sem()
    k.wait("vector", s_nrm, 16)
    for t2 in range(2):
        k.op("vector", "tensor_copy", arr[:, t2 * 512:(t2 + 1) * 512], pb[4 + t2][0:64, :],
             inc=s_ar)
    k.wait("sync", s_ar, 2)
    k.op("sync", "dma_start", out=arr_tok[:], in_=arr[:], transpose=True, inc16=s_ar)
    k.wait("vector", s_ar, 2 + 16)
    for tau in range(8):
        k.op("vector", "max", out=mx8[:], in_=arr_tok[:, tau, :])
        k.op("vector", "drain")
        k.op("vector", "tensor_scalar", routes_tok[:, tau, 0:64], arr_tok[:, tau, :],
             mx8[:, 3:4], 0.25, ALU.is_ge, ALU.mult, inc=s_tk)
        k.op("vector", "drain")
    k.wait("sync", s_tk, 8)
    k.op("sync", "dma_start", out=routes_fm[:].rearrange("p (a b) -> p a b", b=128),
         in_=routes_tok[:], transpose=True, inc16=s_rt)

    s_rp = k.sem(); s_tm = k.sem(); s_v = k.sem(); s_vr = k.sem(); s_h2 = k.sem()
    k.wait("tensor", s_rt, 16)
    for kk in range(8):
        for t2 in range(2):
            rbk = (kk * 2 + t2) % 2
            if kk * 2 + t2 >= 2:
                k.wait("tensor", s_tm, kk * 2 + t2 - 1)
            k.op("tensor", "matmul", pb[rbk][:, :], irep[0:64, kk, :],
                 routes_fm[0:64, t2 * 512:(t2 + 1) * 512], start=True, stop=True,
                 inc=s_rp)
            k.wait("vector", s_rp, kk * 2 + t2 + 1)
            if kk * 2 + t2 >= 2:
                k.wait("vector", s_v, s_v.n)
            k.op("vector", "tensor_mul", tmp[:], vecs[:, kk, t2 * 512:(t2 + 1) * 512],
                 pb[rbk][:, :], inc=s_tm)
            k.wait("tensor", s_tm, kk * 2 + t2 + 1)
            k.op("tensor", "matmul", pb[2 + t2][0:16, :], iselr[:], tmp[:],
                 start=(kk == 0), stop=(kk == 7), inc=(s_v if kk == 7 else None))
    k.wait("vector", s_v, 2)
    for t2 in range(2):
        k.op("vector", "tensor_copy", vfm[:, t2 * 512:(t2 + 1) * 512],
             pb[2 + t2][0:16, :], inc=s_v)
    k.wait("tensor", s_v, 4)
    for t2 in range(2):
        k.op("tensor", "matmul", pb[4 + t2][:, :], ivrep[:], vfm[0:16, t2 * 512:(t2 + 1) * 512],
             start=True, stop=True, inc=s_vr)
    k.wait("vector", s_vr, 2)
    for t2 in range(2):
        k.op("vector", "tensor_copy", vrep_sb[:, t2 * 512:(t2 + 1) * 512],
             pb[4 + t2][:, :], inc=s_vr)
    for kk in range(8):
        for t2 in range(2):
            it = kk * 2 + t2
            rbk = it % 2
            if it >= 2:
                k.wait("tensor", s_h2, it - 1)
            k.op("tensor", "matmul", pb[rbk][:, :], irep[0:64, kk, :],
                 routes_fm[0:64, t2 * 512:(t2 + 1) * 512], start=True, stop=True,
                 inc=s_rp)
            k.wait("vector", s_rp, 16 + it + 1)
            k.wait("vector", s_vr, 4)
            k.op("vector", "tensor_mul", h2[:, kk, t2 * 512:(t2 + 1) * 512],
                 vrep_sb[:, t2 * 512:(t2 + 1) * 512], pb[rbk][:, :], inc=s_h2)

    s_w2 = k.sem(); s_dmm = k.sem(); s_oev = k.sem(); s_ot = k.sem()
    s_fin = k.sem(); s_res = k.sem(); s_rl = k.sem()
    k.wait("tensor", s_h2, 16)
    for n in range(NCH):
        b = n % 2
        if n >= 2:
            k.wait("sync", s_dmm, 2 * (n - 1))
        k.op("sync", "dma_start", out=wts[b][:], in_=w_t[n], inc16=s_w2)
        for t2 in range(2):
            dbk = 4 + (n * 2 + t2) % 4
            k.wait("tensor", s_w2, 16 * (n + 1))
            if n * 2 + t2 >= 4:
                k.wait("tensor", s_oev, n * 2 + t2 - 3)
            for c in range(NCH):
                k.op("tensor", "matmul", pb[dbk][:, :], wts[b][:, c, :],
                     xln[:, c, t2 * 512:(t2 + 1) * 512], start=(c == 0), stop=False)
            for g in range(8):
                k.op("tensor", "matmul", pb[dbk][:, :], bts[:, g, n, :],
                     h2[:, g, t2 * 512:(t2 + 1) * 512], start=False, stop=(g == 7),
                     inc=(s_dmm if g == 7 else None))
            eb = (n * 2 + t2) % 2
            k.wait("scalar", s_dmm, n * 2 + t2 + 1)
            if n * 2 + t2 >= 2:
                k.wait("scalar", s_ot, 16 * (n * 2 + t2 - 1))
            k.op("scalar", "activation", obuf[eb][:], pb[dbk][:, :], AF.Identity,
                 bias=bcol[:, n:n + 1], inc=s_oev)
            k.wait("sync", s_oev, n * 2 + t2 + 1)
            k.op("sync", "dma_start",
                 out=otok[:, t2 * 4:(t2 + 1) * 4, n * 128:(n + 1) * 128],
                 in_=obuf[eb][:], transpose=True, inc16=s_ot)
    for tau in range(8):
        b = tau % 2
        if tau >= 2:
            k.wait("scalar", s_fin, tau - 1)
        k.op("scalar", "dma_start", out=rrl[b][:],
             in_=r_in[tau * 128:(tau + 1) * 128, :], inc16=s_rl)
        k.wait("vector", s_ot, 16 * 32)
        k.wait("vector", s_rl, 16 * (tau + 1))
        if tau >= 1:
            k.wait("vector", s_res, 16 * tau)
        k.op("vector", "drain")
        k.op("vector", "scalar_tensor_tensor", rfin[:], otok[:, tau, :], 0.5,
             rrl[b][:], ALU.mult, ALU.add, inc=s_fin)
        k.wait("sync", s_fin, tau + 1)
        k.op("sync", "dma_start", out=r_out[tau * 128:(tau + 1) * 128, :], in_=rfin[:],
             inc16=s_res)
    k.wait("sync", s_res, 16 * 8)
    return k


def build_mix1a(k=None):
    if k is None:
        k = K()
    r_in9 = k.din("r_in9", [1152, D])
    vec_ins = [k.din(nm, [D]) for nm in
               ("xw_c", "xa_c", "xg_c", "xr_c", "xk_c", "xv_c", "w0v", "a0v")]
    w1 = k.din("w1", [D, 64]); w2 = k.din("w2", [64, D])
    a1 = k.din("a1", [D, 64]); a2 = k.din("a2", [64, D])
    g1 = k.din("g1", [D, 128]); g2 = k.din("g2", [128, D])
    xr_fm = k.dout("xr_fm", [128, NCH, TPC], BF16)
    xk_fm = k.dout("xk_fm", [128, NCH, TPC], BF16)
    xv_fm = k.dout("xv_fm", [128, NCH, TPC], BF16)
    ga_fm = k.dout("ga_fm", [128, NCH, TPC], BF16)
    E1 = k.dout("E1", [128, NCH, TPC], BF16)
    E2 = k.dout("E2", [128, NCH, TPC], BF16)
    E3 = k.dout("E3", [128, NCH, TPC], BF16)
    eqly = k.dout("eqly", [128, NCH, 16])
    g_tok = k.dout("g_tok", [TPC, D], BF16)

    xlnx = k.sb("xlnx", [128, NCH, 1152], BF16)
    xx = k.sb("xx", [128, NCH, TPC], BF16)
    mix = k.sb("mix", [128, NCH, TPC], BF16)
    rres = k.sb("rres", [128, 2, D])
    bnst = k.sb("bnst", [128, 24]); stats = k.sb("stats", [128, 4])
    epscol = k.sb("epscol", [128, 1])
    xlt = k.sb("xlt", [128, D], BF16)
    cols = k.sb("cols", [128, 8 * NCH])
    w1s = k.sb("w1s", [128, NCH, 64], BF16)
    w2s = k.sb("w2s", [64, D], BF16)
    a1s = k.sb("a1s", [128, NCH, 64], BF16)
    a2s = k.sb("a2s", [64, D], BF16)
    g1s = k.sb("g1s", [128, NCH, 128], BF16)
    g2s = k.sb("g2s", [128, NCH, 128], BF16)
    th = k.sb("th", [64, TPC], BF16)
    sgg = k.sb("sgg", [128, TPC], BF16)
    sgm = k.sb("sgm", [128, 512])
    lwch = k.sb("lwch", [128, TPC])
    zer = k.sb("zer", [128, TPC])
    qch = k.sb("qch", [128, TPC])
    qr = k.sb("qr", [128, TPC])
    qx = k.sb("qx", [128, TPC])
    qb = k.sb("qb", [128, 16, 1])
    e1b = k.sb("e1b", [128, TPC], BF16)
    e2b = k.sb("e2b", [128, TPC], BF16)
    e3b = k.sb("e3b", [128, TPC], BF16)
    gab = k.sb("gab", [128, 512], BF16)
    eqly_sb = k.sb("eqly_sb", [128, NCH, 16])
    gbuf = k.sb("gbuf", [128, 512], BF16)
    gst = k.sb("gst", [128, 4, 128], BF16)
    pb = [k.ps(f"pb{i}") for i in range(8)]

    s_c = k.sem()
    for idx, vec in enumerate(vec_ins):
        k.op("sync", "dma_start", out=cols[:, idx * NCH:(idx + 1) * NCH],
             in_=vec.rearrange("(n p) -> p n", p=128),
             allow_slow_non_contiguous=True, inc16=s_c)
    k.op("gpsimd", "dma_start", out=w1s[:], in_=w1.rearrange("(c p) f -> p c f", p=128),
         inc16=s_c)
    k.op("gpsimd", "dma_start", out=w2s[:], in_=w2, inc16=s_c)
    k.op("gpsimd", "dma_start", out=a1s[:], in_=a1.rearrange("(c p) f -> p c f", p=128),
         inc16=s_c)
    k.op("gpsimd", "dma_start", out=a2s[:], in_=a2, inc16=s_c)
    k.op("gpsimd", "dma_start", out=g1s[:], in_=g1.rearrange("(c p) f -> p c f", p=128),
         inc16=s_c)
    k.op("gpsimd", "dma_start", out=g2s[:], in_=g2.rearrange("(c p) f -> p c f", p=128),
         inc16=s_c)
    k.op("vector", "memset", epscol[:], 1e-5)
    k.op("vector", "memset", zer[:], 0.0)
    k.op("vector", "memset", qb[:], 0.0)
    k.op("vector", "drain", inc=s_c)

    s_tp = k.sem()
    emit_ln(k, r_in9, rres, xlnx, (bnst, stats, xlt, epscol), 0, 9, s_tp, nbuf=2)
    k.fence([s_tp, s_c])

    s_xx = k.sem()
    for c in range(NCH):
        k.op("vector", "tensor_sub", xx[:, c, :], xlnx[:, c, 127:1151],
             xlnx[:, c, 128:1152])
    k.op("vector", "drain", inc=s_xx)
    k.fence([s_xx])

    def emit_mix(colidx, sem):
        for c in range(NCH):
            k.op("vector", "scalar_tensor_tensor", mix[:, c, :], xx[:, c, :],
                 cols[:, colidx * NCH + c:colidx * NCH + c + 1], xlnx[:, c, 128:1152],
                 ALU.mult, ALU.add)
        k.op("vector", "drain", inc=sem)
        k.fence([sem])

    def small_up(wsb, nb, sem):
        for t2 in range(2):
            for c in range(NCH):
                k.op("tensor", "matmul", pb[t2][0:nb, :], wsb[:, c, 0:nb],
                     mix[:, c, t2 * 512:(t2 + 1) * 512], start=(c == 0),
                     stop=(c == NCH - 1), inc=(sem if c == NCH - 1 else None))
        k.fence([sem])

    s_m = k.sem()
    emit_mix(0, s_m)
    small_up(w1s, 64, s_m)
    for t2 in range(2):
        k.op("scalar", "activation", th[:, t2 * 512:(t2 + 1) * 512], pb[t2][0:64, :],
             AF.Tanh)
    k.op("scalar", "drain", inc=s_m)
    k.fence([s_m])
    s_e1 = k.sem(); s_e2 = k.sem(); s_e3 = k.sem()
    pbguard = None
    for n in range(NCH):
        s_nn = k.sem()
        if pbguard is not None:
            k.wait("tensor", pbguard[0], pbguard[1])
        for t2 in range(2):
            k.op("tensor", "matmul", pb[2 + t2][:, :], w2s[:, n * 128:(n + 1) * 128],
                 th[:, t2 * 512:(t2 + 1) * 512], start=True, stop=True,
                 inc=(s_nn if t2 == 1 else None))
        k.wait("scalar", s_nn, 1)
        for t2 in range(2):
            k.op("scalar", "activation", sgm[:], pb[2 + t2][:, :], AF.Sigmoid,
                 bias=cols[:, 6 * NCH + n:6 * NCH + n + 1])
            k.op("scalar", "drain", inc=s_nn)
            k.wait("vector", s_nn, 2 + t2)
            k.op("vector", "tensor_scalar", lwch[:, t2 * 512:(t2 + 1) * 512], sgm[:],
                 -0.6065306597126334, None, ALU.mult)
            k.op("vector", "drain", inc=s_nn)
            k.wait("scalar", s_nn, s_nn.n)
        pbguard = (s_nn, s_nn.n)
        k.op("vector", "tensor_tensor_scan", qch[:], lwch[:], zer[:], 0.0,
             ALU.add, ALU.add)
        k.op("vector", "drain")
        k.op("vector", "tensor_copy", qb[:, 1:16, :],
             qch[:].rearrange("p (a b) -> p a b", b=64)[:, 0:15, 63:64])
        k.op("vector", "drain")
        k.op("vector", "tensor_tensor", qr[:].rearrange("p (a b) -> p a b", b=64),
             qch[:].rearrange("p (a b) -> p a b", b=64),
             qb[:].to_broadcast([128, 16, 64]), ALU.subtract)
        k.op("vector", "drain")
        k.op("vector", "tensor_sub", qx[:], qr[:], lwch[:])
        k.op("vector", "tensor_copy", eqly_sb[:, n, :],
             qr[:].rearrange("p (a b) -> p a b", b=64)[:, :, 63:64]
             .rearrange("p a b -> p (a b)"))
        k.op("vector", "drain", inc=s_nn)
        k.wait("scalar", s_nn, 6)
        if n >= 1:
            k.wait("scalar", s_e1, 16 * n)
            k.wait("scalar", s_e2, 16 * n)
            k.wait("scalar", s_e3, 16 * n)
        k.op("scalar", "activation", e1b[:], qr[:], AF.Exp)
        k.op("scalar", "activation", e3b[:], qr[:], AF.Exp, scale=-1.0)
        k.op("scalar", "activation", e2b[:], qx[:], AF.Exp)
        k.op("scalar", "drain", inc=s_nn)
        k.wait("sync", s_nn, 7)
        k.op("sync", "dma_start", out=E1[:, n, :], in_=e1b[:], inc16=s_e1)
        k.op("sync", "dma_start", out=E3[:, n, :], in_=e3b[:], inc16=s_e3)
        k.op("sync", "dma_start", out=E2[:, n, :], in_=e2b[:], inc16=s_e2)
    k.fence([s_e1, s_e2, s_e3])
    s_eq = k.sem()
    k.op("scalar", "activation", eqly_sb[:].rearrange("p a b -> p (a b)"),
         eqly_sb[:].rearrange("p a b -> p (a b)"), AF.Exp)
    k.op("scalar", "drain", inc=s_eq)
    k.wait("sync", s_eq, 1)
    k.op("sync", "dma_start", out=eqly, in_=eqly_sb[:], inc16=s_eq)

    s_a = k.sem()
    emit_mix(1, s_a)
    small_up(a1s, 64, s_a)
    for t2 in range(2):
        k.op("scalar", "activation", th[:, t2 * 512:(t2 + 1) * 512], pb[t2][0:64, :],
             AF.Identity)
    k.op("scalar", "drain", inc=s_a)
    k.fence([s_a])
    s_ga = k.sem()
    pbguard = None
    for n in range(NCH):
        s_nn = k.sem()
        if pbguard is not None:
            k.wait("tensor", pbguard[0], pbguard[1])
        for t2 in range(2):
            k.op("tensor", "matmul", pb[2 + t2][:, :], a2s[:, n * 128:(n + 1) * 128],
                 th[:, t2 * 512:(t2 + 1) * 512], start=True, stop=True,
                 inc=(s_nn if t2 == 1 else None))
        k.wait("scalar", s_nn, 1)
        for t2 in range(2):
            k.op("scalar", "activation", sgm[:], pb[2 + t2][:, :], AF.Sigmoid,
                 bias=cols[:, 7 * NCH + n:7 * NCH + n + 1])
            k.op("scalar", "drain", inc=s_nn)
            k.wait("vector", s_nn, s_nn.n)
            k.op("vector", "tensor_copy", gab[:], sgm[:])
            k.op("vector", "drain", inc=s_nn)
            k.wait("sync", s_nn, s_nn.n)
            k.op("sync", "dma_start", out=ga_fm[:, n, t2 * 512:(t2 + 1) * 512],
                 in_=gab[:], inc16=s_ga)
            k.wait("vector", s_ga, s_ga.n)
            k.wait("scalar", s_ga, s_ga.n)
        pbguard = (s_nn, s_nn.n)
    k.fence([s_ga])

    s_g = k.sem()
    emit_mix(2, s_g)
    for t2 in range(2):
        for c in range(NCH):
            k.op("tensor", "matmul", pb[t2][:, :], g1s[:, c, :],
                 mix[:, c, t2 * 512:(t2 + 1) * 512], start=(c == 0),
                 stop=(c == NCH - 1), inc=(s_g if c == NCH - 1 else None))
    k.fence([s_g])
    for t2 in range(2):
        k.op("scalar", "activation", sgg[:, t2 * 512:(t2 + 1) * 512], pb[t2][:, :],
             AF.Sigmoid)
    k.op("scalar", "drain", inc=s_g)
    k.fence([s_g])
    s_gs = k.sem()
    pbguard = None
    for n in range(NCH):
        s_nn = k.sem()
        if pbguard is not None:
            k.wait("tensor", pbguard[0], pbguard[1])
        for t2 in range(2):
            k.op("tensor", "matmul", pb[2 + t2][:, :], g2s[:, n, :],
                 sgg[:, t2 * 512:(t2 + 1) * 512],
                 start=True, stop=True, inc=(s_nn if t2 == 1 else None))
        k.wait("vector", s_nn, 1)
        for t2 in range(2):
            k.op("vector", "tensor_copy", gbuf[:], pb[2 + t2][:, :])
            k.op("vector", "drain", inc=s_nn)
            k.wait("sync", s_nn, s_nn.n)
            if t2 == 1 or n > 0:
                k.wait("sync", s_gs, s_gs.n)
            k.op("sync", "dma_start", out=gst[:], in_=gbuf[:], transpose=True,
                 inc16=s_nn)
            k.wait("scalar", s_nn, s_nn.n)
            k.op("scalar", "dma_start",
                 out=g_tok[t2 * 512:(t2 + 1) * 512, n * 128:(n + 1) * 128]
                 .rearrange("(a p) f -> p a f", p=128),
                 in_=gst[:], inc16=s_gs)
            k.wait("vector", s_gs, s_gs.n)
        pbguard = (s_nn, s_nn.n)
    k.fence([s_gs])

    for colidx, dst in ((3, xr_fm), (4, xk_fm), (5, xv_fm)):
        s_x = k.sem()
        emit_mix(colidx, s_x)
        for c in range(NCH):
            k.op("sync", "dma_start", out=dst[:, c, :], in_=mix[:, c, :], inc16=s_x)
        k.fence([s_x])
    return k


def build_mix1b(k=None):
    if k is None:
        k = K()
    xr_fm = k.din("xr_fm", [128, NCH, TPC], BF16)
    xk_fm = k.din("xk_fm", [128, NCH, TPC], BF16)
    xv_fm = k.din("xv_fm", [128, NCH, TPC], BF16)
    ga_fm = k.din("ga_fm", [128, NCH, TPC], BF16)
    E1 = k.din("E1", [128, NCH, TPC], BF16)
    E2 = k.din("E2", [128, NCH, TPC], BF16)
    E3 = k.din("E3", [128, NCH, TPC], BF16)
    eqly = k.din("eqly", [128, NCH, 16])
    W_r = k.din("W_r", [D, D]); W_k = k.din("W_k", [D, D]); W_v = k.din("W_v", [D, D])
    kkv = k.din("kkv", [D]); kav = k.din("kav", [D]); rkv = k.din("rkv", [D])
    IND2H = k.din("IND2H", [128, 2], BF16)
    IND2HT = k.din("IND2HT", [2, 128], BF16)
    ah_fm = k.dout("ah_fm", [128, NCH, TPC], BF16)
    bh_fm = k.dout("bh_fm", [128, NCH, TPC], BF16)
    kh_fm = k.dout("kh_fm", [128, NCH, TPC], BF16)
    rh_fm = k.dout("rh_fm", [128, NCH, TPC], BF16)
    vv_tok = k.dout("vv_tok", [TPC, 32, 64], BF16)
    ktl_tok = k.dout("ktl_tok", [TPC, 32, 64], BF16)
    btl_tok = k.dout("btl_tok", [TPC, 32, 64], BF16)
    bonus_tok = k.dout("bonus_tok", [TPC, D], BF16)

    k4s = k.dint("k4s", [128, NCH, TPC], BF16)
    rrs = k.dint("rrs", [128, NCH, TPC], BF16)

    xin = k.sb("xin", [128, NCH, TPC], BF16)
    wcs = k.sb("wcs", [128, NCH, D], BF16)
    ind2h = k.sb("ind2h", [128, 2], BF16)
    ind2ht = k.sb("ind2ht", [2, 128], BF16)
    cols = k.sb("cols", [128, 2 * NCH])
    rkcol = k.sb("rkcol", [128, NCH])
    eqsb = k.sb("eqsb", [128, NCH, 16])
    e1n = k.sb("e1n", [128, TPC], BF16)
    e2n = k.sb("e2n", [128, TPC], BF16)
    e3n = k.sb("e3n", [128, TPC], BF16)
    gan = k.sb("gan", [128, TPC], BF16)
    kf = k.sb("kf", [128, TPC], BF16)
    kk0b = k.sb("kk0b", [128, TPC], BF16)
    sqb = k.sb("sqb", [128, TPC], BF16)
    rnb = k.sb("rnb", [2, TPC], BF16)
    kkb = k.sb("kkb", [128, TPC], BF16)
    t1b = k.sb("t1b", [128, TPC], BF16)
    k4b = k.sb("k4b", [128, TPC], BF16)
    oub = k.sb("oub", [128, TPC], BF16)
    ou2 = k.sb("ou2", [128, TPC], BF16)
    tst = [k.sb(f"tst{i}", [128, 8, 128], BF16) for i in range(2)]
    pb = [k.ps(f"pb{i}") for i in range(8)]

    s_c = k.sem()
    k.op("sync", "dma_start", out=ind2h[:], in_=IND2H, inc16=s_c)
    k.op("sync", "dma_start", out=ind2ht[:], in_=IND2HT, inc16=s_c)
    k.op("sync", "dma_start", out=cols[:, 0:NCH],
         in_=kkv.rearrange("(n p) -> p n", p=128), allow_slow_non_contiguous=True,
         inc16=s_c)
    k.op("sync", "dma_start", out=cols[:, NCH:2 * NCH],
         in_=kav.rearrange("(n p) -> p n", p=128), allow_slow_non_contiguous=True,
         inc16=s_c)
    k.op("sync", "dma_start", out=rkcol[:],
         in_=rkv.rearrange("(n p) -> p n", p=128), allow_slow_non_contiguous=True,
         inc16=s_c)
    k.op("sync", "dma_start", out=eqsb[:], in_=eqly, inc16=s_c)
    k.fence([s_c])

    def proj_phase(W, xsrc, epilogue, tag):
        s_x = k.sem()
        k.op("sync", "dma_start", out=xin[:], in_=xsrc, inc16=s_x)
        s_w = k.sem(); s_mm = k.sem()
        for c in range(NCH):
            k.op("gpsimd", "dma_start", out=wcs[:, c, :],
                 in_=W[c * 128:(c + 1) * 128, :], inc16=s_w)
        k.fence([s_x])
        k.wait("tensor", s_w, s_w.n)
        for n in range(NCH):
            bk0, bk1 = (2 * n) % 4, (2 * n + 1) % 4
            for c in range(NCH):
                for t2 in range(2):
                    k.op("tensor", "matmul", pb[bk0 + t2][:, :],
                         wcs[:, c, n * 128:(n + 1) * 128],
                         xin[:, c, t2 * 512:(t2 + 1) * 512], start=(c == 0),
                         stop=(c == NCH - 1), inc=s_mm)
            k.wait("vector", s_mm, 32 * (n + 1))
            epilogue(n, bk0, bk1)
        return s_mm

    def k_epi(n, bk0, bk1):
        s_n = k.sem()
        k.fence([])
        k.op("sync", "dma_start", out=e2n[:], in_=E2[:, n, :], inc16=s_n)
        k.op("sync", "dma_start", out=e3n[:], in_=E3[:, n, :], inc16=s_n)
        k.op("sync", "dma_start", out=gan[:], in_=ga_fm[:, n, :], inc16=s_n)
        k.wait("vector", s_n, 48)
        for t2 in range(2):
            k.op("vector", "tensor_copy", kf[:, t2 * 512:(t2 + 1) * 512],
                 pb[(bk0, bk1)[t2]][:, :])
        k.op("vector", "drain")
        k.op("vector", "tensor_scalar", kk0b[:], kf[:], cols[:, n:n + 1], None,
             ALU.mult)
        k.op("vector", "drain", inc=s_n)
        k.wait("scalar", s_n, 49)
        k.op("scalar", "activation", sqb[:], kk0b[:], AF.Square)
        k.op("scalar", "drain", inc=s_n)
        k.wait("tensor", s_n, 50)
        for t2 in range(2):
            k.op("tensor", "matmul", pb[4 + t2][0:2, :], ind2h[:],
                 sqb[:, t2 * 512:(t2 + 1) * 512], start=True, stop=True,
                 inc=s_n)
        k.wait("scalar", s_n, 52)
        for t2 in range(2):
            k.op("scalar", "activation", rnb[0:2, t2 * 512:(t2 + 1) * 512],
                 pb[4 + t2][0:2, :], AF.Ln)
        k.op("scalar", "drain")
        k.op("scalar", "activation", rnb[0:2, :], rnb[0:2, :], AF.Exp, scale=-0.5)
        k.op("scalar", "drain", inc=s_n)
        k.wait("vector", s_n, 53)
        k.op("vector", "tensor_scalar", rnb[0:2, :], rnb[0:2, :], 1e12, None, ALU.min)
        k.op("vector", "drain", inc=s_n)
        k.wait("tensor", s_n, 54)
        for t2 in range(2):
            k.op("tensor", "matmul", pb[6 + t2][:, :], ind2ht[0:2, :],
                 rnb[0:2, t2 * 512:(t2 + 1) * 512], start=True, stop=True, inc=s_n)
        k.wait("vector", s_n, 56)
        for t2 in range(2):
            k.op("vector", "tensor_mul", kkb[:, t2 * 512:(t2 + 1) * 512],
                 kk0b[:, t2 * 512:(t2 + 1) * 512], pb[6 + t2][:, :])
        k.op("vector", "drain")
        k.op("vector", "tensor_scalar", t1b[:], gan[:], -1.0, cols[:, NCH + n:NCH + n + 1],
             ALU.add, ALU.mult)
        k.op("vector", "drain")
        k.op("vector", "scalar_tensor_tensor", k4b[:], t1b[:], 1.0, kf[:],
             ALU.add, ALU.mult)
        k.op("vector", "drain")
        k.op("vector", "scalar_tensor_tensor", oub[:], kkb[:], -1.0, e2n[:],
             ALU.mult, ALU.mult)
        k.op("vector", "drain")
        k.op("vector", "tensor_mul", t1b[:], kkb[:], gan[:])
        k.op("vector", "drain")
        k.op("vector", "tensor_mul", ou2[:], t1b[:], e3n[:])
        k.op("vector", "drain", inc=s_n)
        k.wait("sync", s_n, 57)
        k.op("sync", "dma_start", out=ah_fm[:, n, :], in_=oub[:], inc16=s_n)
        k.op("sync", "dma_start", out=bh_fm[:, n, :], in_=ou2[:], inc16=s_n)
        k.op("sync", "dma_start", out=k4s[:, n, :], in_=k4b[:], inc16=s_n)
        k.wait("vector", s_n, s_n.n)
        eqv = eqsb[:, n, :].rearrange("p (a o) -> p a o", o=1)
        k.op("vector", "tensor_tensor", oub[:].rearrange("p (a b) -> p a b", b=64),
             ou2[:].rearrange("p (a b) -> p a b", b=64),
             eqv.to_broadcast([128, 16, 64]), ALU.mult)
        k.op("vector", "drain", inc=s_n)
        k.wait("sync", s_n, s_n.n)
        k.op("sync", "dma_start", out=tst[0][:], in_=oub[:], transpose=True, inc16=s_n)
        k.wait("scalar", s_n, s_n.n)
        for hh_ in range(2):
            k.op("scalar", "dma_start",
                 out=btl_tok[:, 2 * n + hh_, :].rearrange("(a p) d -> p a d", p=128),
                 in_=tst[0][:, :, hh_ * 64:(hh_ + 1) * 64], inc16=s_n)
        k.op("vector", "tensor_mul", ou2[:], k4b[:], e3n[:])
        k.op("vector", "drain", inc=s_n)
        k.wait("sync", s_n, s_n.n)
        k.op("sync", "dma_start", out=kh_fm[:, n, :], in_=ou2[:], inc16=s_n)
        k.op("vector", "tensor_tensor", t1b[:].rearrange("p (a b) -> p a b", b=64),
             ou2[:].rearrange("p (a b) -> p a b", b=64),
             eqv.to_broadcast([128, 16, 64]), ALU.mult)
        k.op("vector", "drain", inc=s_n)
        k.wait("sync", s_n, s_n.n)
        k.op("sync", "dma_start", out=tst[1][:], in_=t1b[:], transpose=True, inc16=s_n)
        k.wait("scalar", s_n, s_n.n)
        for hh_ in range(2):
            k.op("scalar", "dma_start",
                 out=ktl_tok[:, 2 * n + hh_, :].rearrange("(a p) d -> p a d", p=128),
                 in_=tst[1][:, :, hh_ * 64:(hh_ + 1) * 64], inc16=s_n)
        k.fence([s_n])

    proj_phase(W_k, xk_fm, k_epi, "k")

    def r_epi(n, bk0, bk1):
        s_n = k.sem()
        k.op("sync", "dma_start", out=e1n[:], in_=E1[:, n, :], inc16=s_n)
        k.wait("vector", s_n, 16)
        for t2 in range(2):
            k.op("vector", "tensor_copy", kf[:, t2 * 512:(t2 + 1) * 512],
                 pb[(bk0, bk1)[t2]][:, :])
        k.op("vector", "drain")
        k.op("vector", "tensor_mul", oub[:], kf[:], e1n[:])
        k.op("vector", "drain", inc=s_n)
        k.wait("sync", s_n, 17)
        k.op("sync", "dma_start", out=rh_fm[:, n, :], in_=oub[:], inc16=s_n)
        k.op("sync", "dma_start", out=rrs[:, n, :], in_=kf[:], inc16=s_n)
        k.fence([s_n])

    proj_phase(W_r, xr_fm, r_epi, "r")

    def v_epi(n, bk0, bk1):
        s_n = k.sem()
        k.op("sync", "dma_start", out=e1n[:], in_=k4s[:, n, :], inc16=s_n)
        k.op("sync", "dma_start", out=e2n[:], in_=rrs[:, n, :], inc16=s_n)
        k.wait("vector", s_n, 32)
        for t2 in range(2):
            k.op("vector", "tensor_copy", kf[:, t2 * 512:(t2 + 1) * 512],
                 pb[(bk0, bk1)[t2]][:, :])
        k.op("vector", "drain", inc=s_n)
        k.wait("sync", s_n, 33)
        k.op("sync", "dma_start", out=tst[0][:], in_=kf[:], transpose=True, inc16=s_n)
        k.wait("scalar", s_n, s_n.n)
        for hh_ in range(2):
            k.op("scalar", "dma_start",
                 out=vv_tok[:, 2 * n + hh_, :].rearrange("(a p) d -> p a d", p=128),
                 in_=tst[0][:, :, hh_ * 64:(hh_ + 1) * 64], inc16=s_n)
        k.op("vector", "tensor_mul", t1b[:], e1n[:], e2n[:])
        k.op("vector", "drain")
        k.op("vector", "tensor_scalar", t1b[:], t1b[:], rkcol[:, n:n + 1], None,
             ALU.mult)
        k.op("vector", "drain", inc=s_n)
        k.wait("tensor", s_n, s_n.n)
        for t2 in range(2):
            k.op("tensor", "matmul", pb[4 + t2][0:2, :], ind2h[:],
                 t1b[:, t2 * 512:(t2 + 1) * 512], start=True, stop=True, inc=s_n)
        k.wait("vector", s_n, s_n.n)
        k.op("vector", "tensor_copy", rnb[0:2, 0:512], pb[4][0:2, :])
        k.op("vector", "tensor_copy", rnb[0:2, 512:1024], pb[5][0:2, :])
        k.op("vector", "drain", inc=s_n)
        k.wait("tensor", s_n, s_n.n)
        for t2 in range(2):
            k.op("tensor", "matmul", pb[6 + t2][:, :], ind2ht[0:2, :],
                 rnb[0:2, t2 * 512:(t2 + 1) * 512], start=True, stop=True, inc=s_n)
        k.wait("vector", s_n, s_n.n)
        for t2 in range(2):
            k.op("vector", "tensor_mul", oub[:, t2 * 512:(t2 + 1) * 512],
                 kf[:, t2 * 512:(t2 + 1) * 512], pb[6 + t2][:, :])
        k.op("vector", "drain", inc=s_n)
        k.wait("sync", s_n, s_n.n)
        k.op("sync", "dma_start", out=tst[1][:], in_=oub[:], transpose=True, inc16=s_n)
        k.wait("scalar", s_n, s_n.n)
        k.op("scalar", "dma_start",
             out=bonus_tok[:, n * 128:(n + 1) * 128].rearrange("(a p) f -> p a f", p=128),
             in_=tst[1][:], inc16=s_n)
        k.fence([s_n])

    proj_phase(W_v, xv_fm, v_epi, "v")
    return k


def build_scan2(ninst=8, SEQ=4096, k=None):
    NC = SEQ // 64
    if k is None:
        k = K()
    k = K()
    NJ = SEQ // 1024
    ah = k.din("ah", [ninst, NJ, 64, 1024], BF16)
    bh = k.din("bh", [ninst, NJ, 64, 1024], BF16)
    kh = k.din("kh", [ninst, NJ, 64, 1024], BF16)
    rh = k.din("rh", [ninst, NJ, 64, 1024], BF16)
    vt = k.din("vt", [ninst, NJ, 1024, 64], BF16)
    ktl = k.din("ktl", [ninst, NJ, 1024, 64], BF16)
    btl = k.din("btl", [ninst, NJ, 1024, 64], BF16)
    eqc = k.din("eqc", [ninst, NJ, 64, NC // NJ])
    MASKS = k.din("MASKS", [64, 256], BF16)
    o_tok = k.dout("o_tok", [ninst, NJ, 1024, 64], BF16)

    arh = k.sb("arh", [64, NC, 128], BF16)
    bhs = k.sb("bhs", [64, SEQ], BF16)
    khs = k.sb("khs", [64, SEQ], BF16)
    vts = k.sb("vts", [64, NC, 64], BF16)
    kts = k.sb("kts", [64, NC, 64], BF16)
    bts = k.sb("bts", [64, NC, 64], BF16)
    eqs = k.sb("eqs", [64, NC])
    msk = k.sb("msk", [64, 256], BF16)
    o_s = k.sb("o_s", [64, SEQ], BF16)
    SS = [k.sb(f"SS{i}", [64, 64]) for i in range(2)]
    SSb = k.sb("SSb", [64, 64], BF16)
    XL = k.sb("XL", [64, 128], BF16)
    XK = k.sb("XK", [64, 128], BF16)
    P0t = k.sb("P0t", [64, 64], BF16)
    PQ = [k.sb(f"PQ{i}", [64, 2, 64], BF16) for i in range(2)]
    Ucur = [k.sb(f"Ucur{i}", [64, 64], BF16) for i in range(2)]
    W0b = k.sb("W0b", [64, 64], BF16)
    pb = [k.ps(f"pb{i}") for i in range(8)]

    s_c = k.sem()
    k.op("sync", "dma_start", out=msk[:], in_=MASKS, inc16=s_c)
    k.fence([s_c])

    s_ld = k.sem()
    s_n = k.sem()
    s_st = k.sem()
    for i in range(ninst):
        k.op("sync", "dma_start",
             out=arh[:, :, 0:64].rearrange("w (j c) t -> w j c t", c=16),
             in_=ah[i].rearrange("j w (c t) -> w j c t", t=64), inc16=s_ld)
        k.op("sync", "dma_start",
             out=arh[:, :, 64:128].rearrange("w (j c) t -> w j c t", c=16),
             in_=rh[i].rearrange("j w (c t) -> w j c t", t=64), inc16=s_ld)
        k.op("sync", "dma_start", out=bhs[:].rearrange("w (j t) -> w j t", t=1024),
             in_=bh[i].rearrange("j w t -> w j t"), inc16=s_ld)
        k.op("sync", "dma_start", out=khs[:].rearrange("w (j t) -> w j t", t=1024),
             in_=kh[i].rearrange("j w t -> w j t"), inc16=s_ld)
        k.op("sync", "dma_start",
             out=vts[:].rearrange("p (j c) t -> p j c t", c=16),
             in_=vt[i].rearrange("j (c p) t -> p j c t", p=64), inc16=s_ld)
        k.op("sync", "dma_start",
             out=kts[:].rearrange("p (j c) t -> p j c t", c=16),
             in_=ktl[i].rearrange("j (c p) t -> p j c t", p=64), inc16=s_ld)
        k.op("sync", "dma_start",
             out=bts[:].rearrange("p (j c) t -> p j c t", c=16),
             in_=btl[i].rearrange("j (c p) t -> p j c t", p=64), inc16=s_ld)
        k.op("sync", "dma_start", out=eqs[:].rearrange("w (j q) -> w j q", q=NC // NJ),
             in_=eqc[i].rearrange("j w q -> w j q"), inc16=s_ld)
        k.op("vector", "memset", SS[1][:], 0.0)
        k.op("vector", "memset", SSb[:], 0.0)
        k.op("vector", "drain", inc=s_ld)
        k.fence([s_ld])
        for c in range(NC):
            b0 = s_n.n
            csl = slice(c * 64, (c + 1) * 64)
            ahc = arh[:, c, 0:64]
            rhc = arh[:, c, 64:128]
            bhc = bhs[:, csl]
            khc = khs[:, csl]
            vtc = vts[:, c, :]
            k.op("tensor", "matmul", pb[0][0:64, 0:128], bhc, arh[:, c, :],
                 start=True, stop=True)
            k.op("tensor", "matmul", pb[1][0:64, 0:128], khc, arh[:, c, :],
                 start=True, stop=True)
            k.op("tensor", "matmul", pb[2][0:64, 0:64], ahc, bhc,
                 start=True, stop=True, inc=s_n)
            k.wait("vector", s_n, b0 + 1)
            k.op("vector", "tensor_mul", XL[:],
                 pb[0][0:64, 0:128], msk[:, 0:128])
            k.op("vector", "tensor_mul", XK[:],
                 pb[1][0:64, 0:128], msk[:, 0:128])
            k.op("vector", "tensor_mul", P0t[:], pb[2][0:64, 0:64], msk[:, 128:192])
            k.op("vector", "drain", inc=s_n)
            k.wait("tensor", s_n, b0 + 2)
            k.op("tensor", "matmul", pb[3][0:64, 0:64], ahc, SSb[:], start=True,
                 stop=False)
            k.op("tensor", "matmul", pb[3][0:64, 0:64], XK[:, 0:64], vtc,
                 start=False, stop=True, inc=s_n)
            k.wait("scalar", s_n, b0 + 3)
            k.op("scalar", "activation", Ucur[0][:], pb[3][0:64, 0:64], AF.Identity)
            k.op("scalar", "drain", inc=s_n)
            k.wait("vector", s_n, b0 + 4)
            k.op("vector", "tensor_copy", PQ[0][:, 0, :], P0t[:])
            k.op("vector", "tensor_copy", PQ[0][:, 1, :], XL[:, 0:64])
            k.op("vector", "drain", inc=s_n)
            for lv in range(6):
                pq = PQ[lv % 2]
                pqn = PQ[(lv + 1) % 2]
                uc = Ucur[lv % 2]
                ucn = Ucur[(lv + 1) % 2]
                k.wait("tensor", s_n, s_n.n)
                k.op("tensor", "matmul", pb[4][0:64, 0:64], pq[:, 1, :], uc[:],
                     start=True, stop=False)
                k.op("tensor", "matmul", pb[4][0:64, 0:64], msk[:, 192:256], uc[:],
                     start=False, stop=True, inc=(None if lv < 5 else s_n))
                if lv < 5:
                    k.op("tensor", "matmul", pb[5][0:64, 0:64], pq[:, 1, :], pq[:, 0, :],
                         start=True, stop=True)
                    k.op("tensor", "matmul", pb[6][0:64, 0:64], pq[:, 0, :], pq[:, 1, :],
                         start=True, stop=True, inc=s_n)
                k.wait("vector", s_n, s_n.n)
                k.op("vector", "tensor_copy", ucn[:], pb[4][0:64, 0:64])
                if lv < 5:
                    k.op("vector", "tensor_copy", pqn[:, 0, :], pb[5][0:64, 0:64])
                    k.op("vector", "tensor_copy", pqn[:, 1, :], pb[6][0:64, 0:64])
                k.op("vector", "drain", inc=s_n)
            ufin = Ucur[0]
            k.wait("tensor", s_n, s_n.n)
            k.op("tensor", "matmul", pb[7][0:64, 0:64], rhc, SSb[:], start=True,
                 stop=False)
            k.op("tensor", "matmul", pb[7][0:64, 0:64], XL[:, 64:128], ufin[:],
                 start=False, stop=False)
            k.op("tensor", "matmul", pb[7][0:64, 0:64], XK[:, 64:128], vtc,
                 start=False, stop=True, inc=s_n)
            k.wait("scalar", s_n, s_n.n)
            k.op("scalar", "activation", o_s[:, csl], pb[7][0:64, 0:64], AF.Identity)
            k.op("scalar", "drain", inc=s_n)
            k.wait("tensor", s_n, s_n.n - 1)
            k.op("tensor", "matmul", pb[2][0:64, 0:64], bts[:, c, :], ufin[:],
                 start=True, stop=False)
            k.op("tensor", "matmul", pb[2][0:64, 0:64], kts[:, c, :], vtc,
                 start=False, stop=True, inc=s_n)
            k.wait("vector", s_n, s_n.n)
            k.op("vector", "drain")
            k.op("vector", "scalar_tensor_tensor", SS[c % 2][:], SS[(c + 1) % 2][:],
                 eqs[:, c:c + 1], pb[2][0:64, 0:64], ALU.mult, ALU.add)
            k.op("vector", "drain")
            k.op("vector", "tensor_copy", SSb[:], SS[c % 2][:])
            k.op("vector", "drain", inc=s_n)
            k.fence([s_n])
        k.op("sync", "dma_start",
             out=o_tok[i].rearrange("j (c p) d -> p j c d", p=64),
             in_=o_s[:].rearrange("p (j c t) -> p j c t", c=16, t=64), inc16=s_st)
        k.fence([s_st])
    return k


def build_gnb(k=None):
    if k is None:
        k = K()
    o_tok = k.din("o_tok", [TPC, 32, 64], BF16)
    bonus_tok = k.din("bonus_tok", [TPC, D], BF16)
    g_tok = k.din("g_tok", [TPC, D], BF16)
    gam = k.din("gam", [D])
    bet = k.din("bet", [D])
    xog_fm = k.dout("xog_fm", [128, NCH, TPC], BF16)

    onesf = k.sb("onesf", [1, 128])
    grow = k.sb("grow", [1, D])
    brow = k.sb("brow", [1, D])
    grep = k.sb("grep", [128, D])
    brep = k.sb("brep", [128, D])
    ot = [k.sb(f"ot{i}", [128, D], BF16) for i in range(2)]
    bt = [k.sb(f"bt{i}", [128, D], BF16) for i in range(2)]
    gt = [k.sb(f"gt{i}", [128, D], BF16) for i in range(2)]
    sq = k.sb("sq", [128, D], BF16)
    mu = k.sb("mu", [128, 32])
    ss = k.sb("ss", [128, 32])
    rstd = k.sb("rstd", [128, 32, 1])
    muv = k.sb("muv", [128, 32, 1])
    epsc = k.sb("epsc", [128, 1])
    xo = k.sb("xo", [128, D])
    xog = k.sb("xog", [128, D], BF16)
    tst = [k.sb(f"tst{i}", [128, NCH, 128], BF16) for i in range(2)]
    pb = [k.ps(f"pb{i}") for i in range(8)]

    s_c = k.sem()
    k.op("sync", "dma_start", out=grow[:], in_=gam[None, :], inc16=s_c)
    k.op("sync", "dma_start", out=brow[:], in_=bet[None, :], inc16=s_c)
    k.op("vector", "memset", onesf[:], 1.0)
    k.op("vector", "memset", epsc[:], 0.00064)
    k.op("vector", "drain", inc=s_c)
    k.fence([s_c])
    for j in range(4):
        k.op("tensor", "matmul", pb[j][:, :], onesf[:], grow[0:1, j * 512:(j + 1) * 512],
             start=True, stop=True)
        k.op("tensor", "matmul", pb[4 + j][:, :], onesf[:], brow[0:1, j * 512:(j + 1) * 512],
             start=True, stop=True, inc=(s_c if j == 3 else None))
    k.fence([s_c])
    for j in range(4):
        k.op("vector", "tensor_copy", grep[:, j * 512:(j + 1) * 512], pb[j][:, :])
        k.op("vector", "tensor_copy", brep[:, j * 512:(j + 1) * 512], pb[4 + j][:, :])
    k.op("vector", "drain", inc=s_c)
    k.fence([s_c])

    s_st = k.sem()
    for tau in range(8):
        s_n = k.sem()
        b = tau % 2
        sl = slice(tau * 128, (tau + 1) * 128)
        k.op("sync", "dma_start", out=ot[b][:].rearrange("p (h d) -> p h d", d=64),
             in_=o_tok[sl, :, :], inc16=s_n)
        k.op("sync", "dma_start", out=bt[b][:], in_=bonus_tok[sl, :], inc16=s_n)
        k.op("sync", "dma_start", out=gt[b][:], in_=g_tok[sl, :], inc16=s_n)
        k.wait("scalar", s_n, 48)
        k.op("scalar", "activation", sq[:], ot[b][:], AF.Square)
        k.op("scalar", "drain", inc=s_n)
        k.wait("vector", s_n, s_n.n)
        k.op("vector", "reduce_sum", mu[:], ot[b][:].rearrange("p (h i) -> p h i", i=64),
             axis=mybir.AxisListType.X)
        k.op("vector", "reduce_sum", ss[:], sq[:].rearrange("p (h i) -> p h i", i=64),
             axis=mybir.AxisListType.X)
        k.op("vector", "drain")
        k.op("vector", "tensor_scalar", mu[:], mu[:], 1.0 / 64, None, ALU.mult)
        k.op("vector", "drain")
        k.op("vector", "tensor_mul", muv[:, :, 0], mu[:], mu[:])
        k.op("vector", "drain")
        k.op("vector", "scalar_tensor_tensor", ss[:], ss[:], 1.0 / 64,
             muv[:, :, 0], ALU.mult, ALU.subtract)
        k.op("vector", "drain", inc=s_n)
        k.wait("scalar", s_n, s_n.n)
        k.op("scalar", "activation", rstd[:, :, 0], ss[:], AF.Ln, bias=epsc[:, 0:1])
        k.op("scalar", "drain")
        k.op("scalar", "activation", rstd[:, :, 0], rstd[:, :, 0], AF.Exp, scale=-0.5)
        k.op("scalar", "drain", inc=s_n)
        k.wait("vector", s_n, s_n.n)
        k.op("vector", "tensor_copy", muv[:, :, 0], mu[:])
        k.op("vector", "drain")
        k.op("vector", "tensor_tensor", xo[:].rearrange("p (h i) -> p h i", i=64),
             ot[b][:].rearrange("p (h i) -> p h i", i=64),
             muv[:].to_broadcast([128, 32, 64]), ALU.subtract)
        k.op("vector", "drain")
        k.op("vector", "tensor_tensor", xo[:].rearrange("p (h i) -> p h i", i=64),
             xo[:].rearrange("p (h i) -> p h i", i=64),
             rstd[:].to_broadcast([128, 32, 64]), ALU.mult)
        k.op("vector", "drain")
        k.op("vector", "tensor_mul", xo[:], xo[:], grep[:])
        k.op("vector", "drain")
        k.op("vector", "tensor_add", xo[:], xo[:], brep[:])
        k.op("vector", "drain")
        k.op("vector", "tensor_add", xo[:], xo[:], bt[b][:])
        k.op("vector", "drain")
        k.op("vector", "tensor_mul", xog[:], xo[:], gt[b][:])
        k.op("vector", "drain", inc=s_n)
        k.wait("sync", s_n, s_n.n)
        if tau >= 1:
            k.wait("sync", s_st, s_st.n)
        tb = tau % 2
        k.op("sync", "dma_start", out=tst[tb][:], in_=xog[:], transpose=True, inc16=s_n)
        k.wait("scalar", s_n, s_n.n)
        k.op("scalar", "dma_start", out=xog_fm[:, :, tau * 128:(tau + 1) * 128],
             in_=tst[tb][:], inc16=s_st)
        k.fence([s_n])
    k.fence([s_st])
    return k


def kernel_multi(**inp):
    C = host_consts()
    x = np.ascontiguousarray(np.asarray(inp["L0_attn_x"], np.float32).reshape(8192, D))
    zD = np.zeros(D, np.float32)

    def percore(t):
        return [t[c * TPC:(c + 1) * TPC] for c in range(NCORE)]

    def rep(m):
        return [m] * NCORE

    k0 = build_wprep()
    r0 = k0.run(rep({
        "W1": inp["L0_ffn_W1"], "W2": inp["L0_ffn_W2"], "W3": inp["L0_ffn_W3"],
        "A": inp["L1_ffn_A"], "B": inp["L1_ffn_B"], "Wf": inp["L1_ffn_W"],
        "H64S": C["H64S"]})).results

    def ffn0(r):
        k = build_ffn0()
        res = k.run([{"r_in": r[c * TPC:(c + 1) * TPC], "w1t": r0[c]["w1t"],
                      "w3t": r0[c]["w3t"], "w2ft": r0[c]["w2ft"], "H128": C["H128"]}
                     for c in range(NCORE)]).results
        return np.concatenate([res[c]["r_out"] for c in range(NCORE)], axis=0)

    def attn0(r):
        k2 = build_qkv0()
        r2 = k2.run([{"r_in": r[c * TPC:(c + 1) * TPC], "Wqkv": inp["L0_attn_Wqkv"],
                      "bqkv": inp["L0_attn_bqkv"], "slopes": inp["L0_attn_slopes"],
                      "POSC": C["POSC"]} for c in range(NCORE)]).results
        slopes = np.asarray(inp["L0_attn_slopes"], np.float32)

        def gfm(key, b, heads):
            return np.stack([np.concatenate(
                [r2[4 * b + cc][key][:, h, :] for cc in range(4)], axis=1)
                for h in heads], axis=0)

        def gtok(key, b, heads):
            return np.stack([np.concatenate(
                [r2[4 * b + cc][key][:, h * 128:(h + 1) * 128] for cc in range(4)],
                axis=0) for h in heads], axis=0)

        maps3 = []
        for sc in range(NCORE):
            b = sc // 4
            heads = [(sc % 4) * 4 + j for j in range(4)]
            maps3.append({"qd": gfm("qd_fm", b, heads), "kf": gfm("ku_fm", b, heads),
                          "kt": gtok("ku_tok", b, heads), "vt": gtok("v_tok", b, heads),
                          "slp4": slopes[heads], "TRIU": C["TRIU"]})
        k3 = build_lscan()
        r3 = k3.run(maps3).results
        maps4 = []
        for c in range(NCORE):
            b = c // 4
            qtr = c % 4
            o_fm = np.empty((128, NCH, TPC), dtype=BF)
            for h in range(16):
                sc = 4 * b + h // 4
                o_fm[:, h, :] = r3[sc]["o_fm"][h % 4][:, qtr * TPC:(qtr + 1) * TPC]
            maps4.append({"r_in": r[c * TPC:(c + 1) * TPC], "o_in": o_fm,
                          "Wout": inp["L0_attn_Wout"], "bout": inp["L0_attn_bout"]})
        k4 = build_aout0()
        r4 = k4.run(maps4).results
        return np.concatenate([r4[c]["r_out"] for c in range(NCORE)], axis=0)

    def ffn1(r):
        k = build_ffn1()
        res = k.run([{"r_in": r[c * TPC:(c + 1) * TPC], "a_t": r0[c]["a_t"],
                      "b_t": r0[c]["b_t"], "w_t": r0[c]["w_t"],
                      "bias": inp["L1_ffn_bias"], "IND_REP": C["IND_REP"],
                      "IND_NRM": C["IND_NRM"], "IND_SELR": C["IND_SELR"],
                      "IND_VREP": C["IND_VREP"]} for c in range(NCORE)]).results
        return np.concatenate([res[c]["r_out"] for c in range(NCORE)], axis=0)

    def attn1(r):
        k7a = build_mix1a()
        maps = []
        for c in range(NCORE):
            if c % 4 == 0:
                prev = np.zeros((1, D), np.float32)
            else:
                prev = r[c * TPC - 1:c * TPC]
            r9 = np.concatenate([np.broadcast_to(prev, (128, D)),
                                 r[c * TPC:(c + 1) * TPC]], axis=0)
            maps.append({"r_in9": r9.copy(),
                         "xw_c": inp["L1_attn_x_w"], "xa_c": inp["L1_attn_x_a"],
                         "xg_c": inp["L1_attn_x_g"], "xr_c": inp["L1_attn_x_r"],
                         "xk_c": inp["L1_attn_x_k"], "xv_c": inp["L1_attn_x_v"],
                         "w0v": inp["L1_attn_w0"], "a0v": inp["L1_attn_a0"],
                         "w1": inp["L1_attn_w1"], "w2": inp["L1_attn_w2"],
                         "a1": inp["L1_attn_a1"], "a2": inp["L1_attn_a2"],
                         "g1": inp["L1_attn_g1"], "g2": inp["L1_attn_g2"]})
        r7a = k7a.run(maps).results
        k7b = build_mix1b()
        maps = []
        for c in range(NCORE):
            m = {kk: r7a[c][kk] for kk in
                 ["xr_fm", "xk_fm", "xv_fm", "ga_fm", "E1", "E2", "E3", "eqly"]}
            m.update({"W_r": inp["L1_attn_W_r"], "W_k": inp["L1_attn_W_k"],
                      "W_v": inp["L1_attn_W_v"], "kkv": inp["L1_attn_k_k"],
                      "kav": inp["L1_attn_k_a"],
                      "rkv": np.ascontiguousarray(
                          np.asarray(inp["L1_attn_r_k"], np.float32).reshape(-1)),
                      "IND2H": C["IND2H"], "IND2HT": C["IND2HT"]})
            maps.append(m)
        r7b = k7b.run(maps).results

        def gfm(key, src, b, h):
            return np.concatenate(
                [src[4 * b + cc][key][(h % 2) * 64:(h % 2) * 64 + 64, h // 2, :]
                 for cc in range(4)], axis=1)

        def gtok(key, src, b, h):
            return np.concatenate(
                [src[4 * b + cc][key][:, h * 64:(h + 1) * 64] for cc in range(4)],
                axis=0)

        maps = []
        for sc in range(NCORE):
            b = sc // 4
            heads = [(sc % 4) * 8 + j for j in range(8)]
            maps.append({
                "ah": np.stack([gfm("ah_fm", r7b, b, h) for h in heads]),
                "bh": np.stack([gfm("bh_fm", r7b, b, h) for h in heads]),
                "kh": np.stack([gfm("kh_fm", r7b, b, h) for h in heads]),
                "rh": np.stack([gfm("rh_fm", r7b, b, h) for h in heads]),
                "vt": np.stack([gtok("vv_tok", r7b, b, h) for h in heads]),
                "ktl": np.stack([gtok("ktl_tok", r7b, b, h) for h in heads]),
                "btl": np.stack([gtok("btl_tok", r7b, b, h) for h in heads]),
                "eqc": np.stack([gfm("eqly", r7a, b, h) for h in heads]).astype(np.float32),
                "MASKS": C["MASKS"]})
        k8 = build_scan2(ninst=8)
        r8 = k8.run(maps).results
        maps = []
        for c in range(NCORE):
            b = c // 4
            qtr = c % 4
            o_tok = np.empty((TPC, D), dtype=BF)
            for h in range(32):
                sc = 4 * b + h // 8
                o_tok[:, h * 64:(h + 1) * 64] = \
                    r8[sc]["o_tok"][h % 8][qtr * TPC:(qtr + 1) * TPC, :]
            maps.append({"o_tok": o_tok, "bonus_tok": r7b[c]["bonus_tok"],
                         "g_tok": r7a[c]["g_tok"], "gam": inp["L1_attn_ln_gamma"],
                         "bet": inp["L1_attn_ln_beta"]})
        k9 = build_gnb()
        r9 = k9.run(maps).results
        k4o = build_aout0()
        maps = [{"r_in": r[c * TPC:(c + 1) * TPC], "o_in": r9[c]["xog_fm"],
                 "Wout": inp["L1_attn_W_o"], "bout": zD} for c in range(NCORE)]
        r4o = k4o.run(maps).results
        return np.concatenate([r4o[c]["r_out"] for c in range(NCORE)], axis=0)

    r = ffn0(x)
    r = attn0(r)
    r = ffn0(r)
    r = ffn1(r)
    r = attn1(r)
    r = ffn1(r)
    return r.reshape(2, 4096, D)


def emit_ag(k, pairs):
    rg = [list(range(NCORE))]
    s_cc = k.sem()
    for (i_ap, o_ap) in pairs:
        k.raw("gpsimd", lambda e, i_ap=i_ap, o_ap=o_ap, s=s_cc: e.collective_compute(
            "AllGather", ALU.bypass, replica_groups=rg,
            ins=[i_ap.opt()], outs=[o_ap.opt()]).then_inc(s.h, 1))
        s_cc.n += 1
    k.wait("gpsimd", s_cc, s_cc.n)
    k.wait("sync", s_cc, s_cc.n)
    return s_cc


def build_fused():
    k = K()
    nc = k.nc
    xc = k.din("xc", [TPC, D])
    W = {n: k.din(n, sh) for n, sh in [
        ("L0_ffn_W1", [HID, D]), ("L0_ffn_W2", [D, HID]), ("L0_ffn_W3", [HID, D]),
        ("L1_ffn_A", [64, 16, D]), ("L1_ffn_B", [64, D, 16]), ("L1_ffn_W", [D, D]),
        ("L0_attn_Wqkv", [D, 3 * D]), ("L0_attn_bqkv", [3 * D]),
        ("L0_attn_Wout", [D, D]), ("L0_attn_bout", [D]), ("L0_attn_slopes", [16]),
        ("L1_ffn_bias", [D]),
        ("L1_attn_x_r", [D]), ("L1_attn_x_w", [D]), ("L1_attn_x_k", [D]),
        ("L1_attn_x_v", [D]), ("L1_attn_x_a", [D]), ("L1_attn_x_g", [D]),
        ("L1_attn_w0", [D]), ("L1_attn_a0", [D]), ("L1_attn_k_k", [D]),
        ("L1_attn_k_a", [D]), ("L1_attn_ln_gamma", [D]), ("L1_attn_ln_beta", [D]),
        ("L1_attn_w1", [D, 64]), ("L1_attn_w2", [64, D]),
        ("L1_attn_a1", [D, 64]), ("L1_attn_a2", [64, D]),
        ("L1_attn_g1", [D, 128]), ("L1_attn_g2", [128, D]),
        ("L1_attn_rk", [D]),
        ("L1_attn_W_r", [D, D]), ("L1_attn_W_k", [D, D]), ("L1_attn_W_v", [D, D]),
        ("L1_attn_W_o", [D, D]), ("zb", [D]),
    ]}
    CN = {n: k.din(n, sh, BF16) for n, sh in [
        ("H128", [128, 128]), ("H64S", [64, 64]), ("MASKS", [64, 256]),
        ("IND_REP", [8, 64, 128]), ("IND_NRM", [8, 128, 64]),
        ("IND_SELR", [128, 16]), ("IND_VREP", [16, 128]),
        ("IND2H", [128, 2]), ("IND2HT", [2, 128]), ("MASKS2", [64, 320]),
    ]}
    CN["TRIU"] = k.din("TRIU", [128, 128], U8)
    CN["POSC"] = k.din("POSC", [128, 16])
    slp4c = k.din("slp4c", [4])
    I32 = mybir.dt.int32
    IXL0F = k.din("IXL0F", [128, 1], I32)
    IXL0T = k.din("IXL0T", [256, 1], I32)
    IXO0 = k.din("IXO0", [128, 1], I32)
    IXPV = k.din("IXPV", [1, 1], I32)
    BMASK = k.din("BMASK", [1, 1])
    IXL1F = k.din("IXL1F", [128, 1], I32)
    IXL1T = k.din("IXL1T", [128, 1], I32)
    IXEQ = k.din("IXEQ", [32, 1], I32)
    IXO1 = k.din("IXO1", [128, 1], I32)
    out_r = k.dout("out_r", [TPC, D])

    w1t = k.dint("w1t", [NHCH, 128, NCH, 128], BF16)
    w3t = k.dint("w3t", [NHCH, 128, NCH, 128], BF16)
    w2ft = k.dint("w2ft", [NHCH, 128, D], BF16)
    a_t = k.dint("a_t", [8, 128, NCH, 128], BF16)
    w_t = k.dint("w_t", [NCH, 128, NCH, 128], BF16)
    b_t = k.dint("b_t", [8, 128, NCH, 128], BF16)
    R = [k.dint(f"R{i}", [TPC, D]) for i in range(5)]
    qd_hm = k.dint("qd_hm", [16, 128, TPC], BF16)
    ku_hm = k.dint("ku_hm", [16, 128, TPC], BF16)
    kt_hm = k.dint("kt_hm", [16, TPC, 128], BF16)
    vt_hm = k.dint("vt_hm", [16, TPC, 128], BF16)
    lq = k.dint("lq", [4, 4, 128, TPC], BF16)
    lkf = k.dint("lkf", [4, 4, 128, TPC], BF16)
    lkt = k.dint("lkt", [4, 4, TPC, 128], BF16)
    lvt = k.dint("lvt", [4, 4, TPC, 128], BF16)
    o0_lay = k.dint("o0_lay", [4, 4, 128, TPC], BF16)
    oin0 = k.dint("oin0", [16, 128, TPC], BF16)
    mix_fm = {n: k.dint(n, [NCH, 128, TPC], BF16) for n in
              ["xr_fm", "xk_fm", "xv_fm", "ga_fm", "E1", "E2", "E3"]}
    eq_lay = k.dint("eq_lay", [NCH, 128, 16])
    g_tok = k.dint("g_tok", [TPC, D], BF16)
    r4p = k.dint("r4p", [1152, D])
    m1b = {n: k.dint(n, [NCH, 128, TPC], BF16) for n in
           ["ah_fm", "bh_fm", "kh_fm", "rh_fm"]}
    m1t = {n: k.dint(n, [32, TPC, 64], BF16) for n in
           ["vv_tok", "ktl_tok", "btl_tok"]}
    bonus_tok = k.dint("bonus_tok", [TPC, D], BF16)
    sc_fm = {n: k.dint("s_" + n, [8, 4, 64, TPC], BF16) for n in
             ["ah", "bh", "kh", "rh"]}
    sc_tok = {n: k.dint("s_" + n, [8, 4, TPC, 64], BF16) for n in
              ["vt", "ktl", "btl"]}
    s_eqc = k.dint("s_eqc", [8, 4, 64, 16])
    o8_lay = k.dint("o8_lay", [8, 4, TPC, 64], BF16)
    otok1_lay = k.dint("otok1_lay", [32, TPC, 64], BF16)
    xog_fm = k.dint("xog_fm", [128, NCH, TPC], BF16)
    AGQ = k.dint("AGQ", [8 * 16 * 128, TPC], BF16, addr_space="Shared")
    AGKF = k.dint("AGKF", [8 * 16 * 128, TPC], BF16, addr_space="Shared")
    AGKT = k.dint("AGKT", [8 * 16 * TPC, 128], BF16, addr_space="Shared")
    AGVT = k.dint("AGVT", [8 * 16 * TPC, 128], BF16, addr_space="Shared")
    AGO0 = k.dint("AGO0", [8 * 16 * 128, TPC], BF16, addr_space="Shared")
    AGROW = k.dint("AGROW", [8, D], addr_space="Shared")
    AGL = {n: k.dint("AG" + n, [8 * 32 * 64, TPC], BF16, addr_space="Shared")
           for n in ["ah", "bh", "kh", "rh"]}
    AGT = {n: k.dint("AG" + n, [8 * 32 * TPC, 64], BF16, addr_space="Shared")
           for n in ["vv", "ktl", "btl"]}
    AGEQ = k.dint("AGEQ", [8 * 32 * 64, 16], addr_space="Shared")
    AGO1 = k.dint("AGO1", [8 * 32 * TPC, 64], BF16, addr_space="Shared")

    k.mark_dram()

    def seg(io, builder, *a, **kw):
        k.seg_begin(io)
        builder(*a, k=k, **kw)
        k.seg_run()

    _rg_cache = {}

    def regather(s_g, srcAG, run_el, dt, dst_flat, idx_dram, nrows):
        key = (id(k._seg_es), str(dt))
        if key not in _rg_cache:
            _rg_cache[key] = (k.sb("gst", [128, 16384], dt),
                              k.sb("gix", [128, 1], mybir.dt.int32))
        stage_full, ix = _rg_cache[key]
        stage = stage_full[:, 0:run_el]
        for g in range((nrows + 127) // 128):
            n = min(128, nrows - g * 128)
            k.op("sync", "dma_start", out=ix[0:n, :],
                 in_=idx_dram[g * 128:g * 128 + n, :], inc16=s_g)
            k.wait("gpsimd", s_g, s_g.n)
            def fg(e, n=n):
                e.indirect_dma_start(
                    out=stage[0:n, :], out_offset=None, in_=srcAG,
                    in_offset=bass.IndirectOffsetOnAxis(ix[0:n, 0:1], 0)
                ).then_inc(s_g.h, 16)
            k.raw("gpsimd", fg)
            s_g.n += 16
            k.wait("sync", s_g, s_g.n)
            k.op("sync", "dma_start",
                 out=dst_flat[g * 128:g * 128 + n, :], in_=stage[0:n, :], inc16=s_g)
            k.wait("gpsimd", s_g, s_g.n)
        k.wait("sync", s_g, s_g.n)

    seg({"W1": W["L0_ffn_W1"], "W2": W["L0_ffn_W2"], "W3": W["L0_ffn_W3"],
         "A": W["L1_ffn_A"], "B": W["L1_ffn_B"], "Wf": W["L1_ffn_W"],
         "H64S": CN["H64S"], "w1t": w1t, "w3t": w3t, "w2ft": w2ft,
         "a_t": a_t, "w_t": w_t, "b_t": b_t}, build_wprep)

    def io_ffn0(rin, rout):
        return {"r_in": rin, "w1t": w1t, "w3t": w3t, "w2ft": w2ft,
                "H128": CN["H128"], "r_out": rout}

    seg(io_ffn0(xc, R[0]), build_ffn0)
    seg({"r_in": R[0], "Wqkv": W["L0_attn_Wqkv"], "bqkv": W["L0_attn_bqkv"],
         "slopes": W["L0_attn_slopes"], "POSC": CN["POSC"],
         "qd_fm": qd_hm.rearrange("h p t -> p h t"),
         "ku_fm": ku_hm.rearrange("h p t -> p h t"),
         "ku_tok": kt_hm.rearrange("h t d -> t h d"),
         "v_tok": vt_hm.rearrange("h t d -> t h d")}, build_qkv0)

    k.seg_begin({})
    emit_ag(k, [(qd_hm.rearrange("h p t -> (h p) t"), AGQ),
                (ku_hm.rearrange("h p t -> (h p) t"), AGKF),
                (kt_hm.rearrange("h t d -> (h t) d"), AGKT),
                (vt_hm.rearrange("h t d -> (h t) d"), AGVT)])
    s_g = k.sem()
    for (AGsrc, dst) in ((AGQ, lq), (AGKF, lkf)):
        regather(s_g, AGsrc.rearrange("(z s) t -> z (s t)", s=16), 16 * TPC, BF16,
                 dst.rearrange("i j p t -> (i j p) t")
                 .rearrange("(z s) t -> z (s t)", s=16), IXL0F, 128)
    for (AGsrc, dst) in ((AGKT, lkt), (AGVT, lvt)):
        regather(s_g, AGsrc.rearrange("(z s) d -> z (s d)", s=64), 64 * 128, BF16,
                 dst.rearrange("i j t d -> (i j t) d")
                 .rearrange("(z s) d -> z (s d)", s=64), IXL0T, 256)
    k.seg_run()

    seg({"qd": lq, "kf": lkf, "kt": lkt, "vt": lvt,
         "slp4": slp4c, "TRIU": CN["TRIU"], "o_fm": o0_lay}, build_lscan)

    k.seg_begin({})
    emit_ag(k, [(o0_lay.rearrange("i q p t -> (i q p) t"), AGO0)])
    s_g = k.sem()
    regather(s_g, AGO0.rearrange("(z s) t -> z (s t)", s=16), 16 * TPC, BF16,
             oin0.rearrange("h p t -> (h p) t").rearrange("(z s) t -> z (s t)", s=16),
             IXO0, 128)
    k.seg_run()

    seg({"r_in": R[0], "o_in": oin0.rearrange("h p t -> p h t"),
         "Wout": W["L0_attn_Wout"], "bout": W["L0_attn_bout"], "r_out": R[1]},
        build_aout0)
    seg(io_ffn0(R[1], R[2]), build_ffn0)

    def io_ffn1(rin, rout):
        return {"r_in": rin, "a_t": a_t, "b_t": b_t, "w_t": w_t,
                "bias": W["L1_ffn_bias"], "IND_REP": CN["IND_REP"],
                "IND_NRM": CN["IND_NRM"], "IND_SELR": CN["IND_SELR"],
                "IND_VREP": CN["IND_VREP"], "r_out": rout}

    seg(io_ffn1(R[2], R[3]), build_ffn1)

    k.seg_begin({})
    s_g = k.sem()
    lrow = k.sb("lrow", [1, D])
    ix_sb = k.sb("ix_sb", [1, 1], mybir.dt.int32)
    bm_sb = k.sb("bm_sb", [1, 1])
    prow = k.sb("prow", [1, D])
    agin = k.dint("agrow_in", [1, D])
    k.op("sync", "dma_start", out=lrow[:], in_=R[3][TPC - 1:TPC, :], inc16=s_g)
    k.op("sync", "dma_start", out=ix_sb[:], in_=IXPV, inc16=s_g)
    k.op("sync", "dma_start", out=bm_sb[:], in_=BMASK, inc16=s_g)
    k.wait("sync", s_g, s_g.n)
    k.op("sync", "dma_start", out=agin, in_=lrow[:], inc16=s_g)
    k.wait("gpsimd", s_g, s_g.n)
    emit_ag(k, [(agin, AGROW)])
    k.wait("gpsimd", s_g, s_g.n)
    def f(e):
        rp = e.value_load(ix_sb[0:1, 0:1])
        e.dma_start(out=prow[:], in_=AGROW[bass.ts(rp, 1), :]).then_inc(s_g.h, 16)
    k.raw("gpsimd", f)
    s_g.n += 16
    k.wait("vector", s_g, s_g.n)
    s_p = k.sem()
    k.op("vector", "tensor_scalar", prow[:], prow[:], bm_sb[0:1, 0:1], None, ALU.mult)
    k.op("vector", "drain", inc=s_p)
    k.wait("sync", s_p, s_p.n)
    k.op("sync", "dma_start", out=r4p[127:128, :], in_=prow[:], inc16=s_p)
    k.op("sync", "dma_start", out=r4p[0:127, :], in_=R[3][0:127, :], inc16=s_p)
    k.op("sync", "dma_start", out=r4p[128:1152, :], in_=R[3], inc16=s_p)
    k.wait("sync", s_p, s_p.n)
    k.seg_run()

    mixio = {n: mix_fm[n].rearrange("n p t -> p n t") for n in mix_fm}
    seg({"r_in9": r4p, "xw_c": W["L1_attn_x_w"], "xa_c": W["L1_attn_x_a"],
         "xg_c": W["L1_attn_x_g"], "xr_c": W["L1_attn_x_r"],
         "xk_c": W["L1_attn_x_k"], "xv_c": W["L1_attn_x_v"],
         "w0v": W["L1_attn_w0"], "a0v": W["L1_attn_a0"],
         "w1": W["L1_attn_w1"], "w2": W["L1_attn_w2"], "a1": W["L1_attn_a1"],
         "a2": W["L1_attn_a2"], "g1": W["L1_attn_g1"], "g2": W["L1_attn_g2"],
         **mixio, "eqly": eq_lay.rearrange("n p q -> p n q"), "g_tok": g_tok},
        build_mix1a)
    seg({**mixio, "eqly": eq_lay.rearrange("n p q -> p n q"),
         "W_r": W["L1_attn_W_r"], "W_k": W["L1_attn_W_k"], "W_v": W["L1_attn_W_v"],
         "kkv": W["L1_attn_k_k"], "kav": W["L1_attn_k_a"], "rkv": W["L1_attn_rk"],
         "IND2H": CN["IND2H"], "IND2HT": CN["IND2HT"],
         "ah_fm": m1b["ah_fm"].rearrange("n p t -> p n t"),
         "bh_fm": m1b["bh_fm"].rearrange("n p t -> p n t"),
         "kh_fm": m1b["kh_fm"].rearrange("n p t -> p n t"),
         "rh_fm": m1b["rh_fm"].rearrange("n p t -> p n t"),
         "vv_tok": m1t["vv_tok"].rearrange("h t d -> t h d"),
         "ktl_tok": m1t["ktl_tok"].rearrange("h t d -> t h d"),
         "btl_tok": m1t["btl_tok"].rearrange("h t d -> t h d"),
         "bonus_tok": bonus_tok}, build_mix1b)

    k.seg_begin({})
    emit_ag(k, [(m1b["ah_fm"].rearrange("n p t -> (n p) t"), AGL["ah"]),
                (m1b["bh_fm"].rearrange("n p t -> (n p) t"), AGL["bh"]),
                (m1b["kh_fm"].rearrange("n p t -> (n p) t"), AGL["kh"]),
                (m1b["rh_fm"].rearrange("n p t -> (n p) t"), AGL["rh"]),
                (m1t["vv_tok"].rearrange("h t d -> (h t) d"), AGT["vv"]),
                (m1t["ktl_tok"].rearrange("h t d -> (h t) d"), AGT["ktl"]),
                (m1t["btl_tok"].rearrange("h t d -> (h t) d"), AGT["btl"]),
                (eq_lay.rearrange("n p q -> (n p) q"), AGEQ)])
    s_g = k.sem()
    for nm in ["ah", "bh", "kh", "rh"]:
        regather(s_g, AGL[nm].rearrange("(z s) t -> z (s t)", s=16), 16 * TPC, BF16,
                 sc_fm[nm].rearrange("i j w t -> (i j w) t")
                 .rearrange("(z s) t -> z (s t)", s=16), IXL1F, 128)
    for nm, dst in (("vv", sc_tok["vt"]), ("ktl", sc_tok["ktl"]),
                    ("btl", sc_tok["btl"])):
        regather(s_g, AGT[nm].rearrange("(z s) d -> z (s d)", s=256), 256 * 64, BF16,
                 dst.rearrange("i j t d -> (i j t) d")
                 .rearrange("(z s) d -> z (s d)", s=256), IXL1T, 128)
    regather(s_g, AGEQ.rearrange("(z s) q -> z (s q)", s=64), 64 * 16, F32,
             s_eqc.rearrange("i j w q -> (i j w) q")
             .rearrange("(z s) q -> z (s q)", s=64), IXEQ, 32)
    k.seg_run()

    seg({"ah": sc_fm["ah"], "bh": sc_fm["bh"], "kh": sc_fm["kh"],
         "rh": sc_fm["rh"], "vt": sc_tok["vt"], "ktl": sc_tok["ktl"],
         "btl": sc_tok["btl"], "eqc": s_eqc, "MASKS": CN["MASKS"],
         "MASKS2": CN["MASKS2"], "o_tok": o8_lay}, build_scan2_v2, ninst=8)

    k.seg_begin({})
    emit_ag(k, [(o8_lay.rearrange("i q t d -> (i q t) d"), AGO1)])
    s_g = k.sem()
    regather(s_g, AGO1.rearrange("(z s) d -> z (s d)", s=256), 256 * 64, BF16,
             otok1_lay.rearrange("h t d -> (h t) d")
             .rearrange("(z s) d -> z (s d)", s=256), IXO1, 128)
    k.seg_run()

    seg({"o_tok": otok1_lay.rearrange("h t d -> t h d"), "bonus_tok": bonus_tok,
         "g_tok": g_tok, "gam": W["L1_attn_ln_gamma"], "bet": W["L1_attn_ln_beta"],
         "xog_fm": xog_fm}, build_gnb)
    seg({"r_in": R[3], "o_in": xog_fm, "Wout": W["L1_attn_W_o"], "bout": W["zb"],
         "r_out": R[4]}, build_aout0)
    k.seg_begin(io_ffn1(R[4], out_r))
    build_ffn1(k=k)
    streams = k.streams
    with nc.Block() as block:
        @block.tensor
        def _(e):
            for f in streams["tensor"]:
                f(e)

        @block.vector
        def _(e):
            for f in streams["vector"]:
                f(e)

        @block.scalar
        def _(e):
            for f in streams["scalar"]:
                f(e)

        @block.gpsimd
        def _(e):
            for f in streams["gpsimd"]:
                f(e)

        @block.sync
        def _(e):
            for f in streams["sync"]:
                f(e)
    k._seg_es.close()
    return k


def kernel_fused(**inp):
    C = host_consts()
    x = np.ascontiguousarray(np.asarray(inp["L0_attn_x"], np.float32).reshape(8192, D))
    slopes = np.asarray(inp["L0_attn_slopes"], np.float32)
    k = build_fused()
    maps = []
    for c in range(NCORE):
        b, qtr = c // 4, c % 4
        heads0 = [qtr * 4 + j for j in range(4)]
        ixl0f = np.zeros(128, np.int32)
        ixl0t = np.zeros(256, np.int32)
        for i, h in enumerate(heads0):
            for j in range(4):
                r = 4 * b + j
                for sub in range(8):
                    ixl0f[(i * 4 + j) * 8 + sub] = (r * 16 + h) * 8 + sub
                for sub in range(16):
                    ixl0t[(i * 4 + j) * 16 + sub] = (r * 16 + h) * 16 + sub
        ixo0 = np.zeros(128, np.int32)
        for h in range(16):
            sc = 4 * b + h // 4
            for sub in range(8):
                ixo0[h * 8 + sub] = ((sc * 4 + h % 4) * 4 + qtr) * 8 + sub
        heads1 = [qtr * 8 + j for j in range(8)]
        ixl1f = np.zeros(128, np.int32)
        ixl1t = np.zeros(128, np.int32)
        ixeq = np.zeros(32, np.int32)
        for i, h in enumerate(heads1):
            for j in range(4):
                r = 4 * b + j
                for sub in range(4):
                    ixl1f[(i * 4 + j) * 4 + sub] = (r * 32 + h) * 4 + sub
                    ixl1t[(i * 4 + j) * 4 + sub] = (r * 32 + h) * 4 + sub
                if j == 0:
                    pass
            for j in range(4):
                ixeq[i * 4 + j] = (4 * b + j) * 32 + h
        ixo1 = np.zeros(128, np.int32)
        for h in range(32):
            sc = 4 * b + h // 8
            for sub in range(4):
                ixo1[h * 4 + sub] = (((sc * 8 + h % 8) * 4 + qtr)) * 4 + sub
        m = {
            "xc": x[c * TPC:(c + 1) * TPC], "slp4c": slopes[heads0],
            "IXL0F": ixl0f[:, None], "IXL0T": ixl0t[:, None], "IXO0": ixo0[:, None],
            "IXPV": np.array([[(c - 1) % 8]], np.int32),
            "BMASK": np.array([[0.0 if c % 4 == 0 else 1.0]], np.float32),
            "IXL1F": ixl1f[:, None], "IXL1T": ixl1t[:, None],
            "IXEQ": ixeq[:, None], "IXO1": ixo1[:, None],
            "zb": np.zeros(D, np.float32),
            "L1_attn_rk": np.ascontiguousarray(
                np.asarray(inp["L1_attn_r_k"], np.float32).reshape(-1)),
            "H128": C["H128"], "H64S": C["H64S"], "MASKS": C["MASKS"],
            "IND_REP": C["IND_REP"], "IND_NRM": C["IND_NRM"],
            "IND_SELR": C["IND_SELR"], "IND_VREP": C["IND_VREP"],
            "IND2H": C["IND2H"], "IND2HT": C["IND2HT"], "TRIU": C["TRIU"],
            "POSC": C["POSC"], "MASKS2": C["MASKS2"],
        }
        for n in ["L0_ffn_W1", "L0_ffn_W2", "L0_ffn_W3", "L1_ffn_A", "L1_ffn_B",
                  "L1_ffn_W", "L0_attn_Wqkv", "L0_attn_bqkv", "L0_attn_Wout",
                  "L0_attn_bout", "L0_attn_slopes", "L1_ffn_bias",
                  "L1_attn_x_r", "L1_attn_x_w", "L1_attn_x_k", "L1_attn_x_v",
                  "L1_attn_x_a", "L1_attn_x_g", "L1_attn_w0", "L1_attn_a0",
                  "L1_attn_k_k", "L1_attn_k_a", "L1_attn_ln_gamma",
                  "L1_attn_ln_beta", "L1_attn_w1", "L1_attn_w2", "L1_attn_a1",
                  "L1_attn_a2", "L1_attn_g1", "L1_attn_g2", "L1_attn_W_r",
                  "L1_attn_W_k", "L1_attn_W_v", "L1_attn_W_o"]:
            m[n] = inp[n]
        maps.append(m)
    res = k.launch(maps)
    out = np.concatenate([res.results[c]["out_r"] for c in range(NCORE)], axis=0)
    _ = (inp["L1_attn_v0"].shape, inp["L1_attn_v1"].shape, inp["L1_attn_v2"].shape)
    return out.reshape(2, 4096, D)


def kernel(**inp):
    return kernel_fused(**inp)


def build_scan2_v2(ninst=8, SEQ=4096, k=None):
    if k is None:
        k = K()
    NC = SEQ // 64
    NJ = SEQ // 1024
    NG = NC // 8
    ah = k.din("ah", [ninst, NJ, 64, 1024], BF16)
    bh = k.din("bh", [ninst, NJ, 64, 1024], BF16)
    kh = k.din("kh", [ninst, NJ, 64, 1024], BF16)
    rh = k.din("rh", [ninst, NJ, 64, 1024], BF16)
    vt = k.din("vt", [ninst, NJ, 1024, 64], BF16)
    ktl = k.din("ktl", [ninst, NJ, 1024, 64], BF16)
    btl = k.din("btl", [ninst, NJ, 1024, 64], BF16)
    eqc = k.din("eqc", [ninst, NJ, 64, NC // NJ])
    MASKS = k.din("MASKS", [64, 256], BF16)
    MASKS2 = k.din("MASKS2", [64, 320], BF16)
    o_tok = k.dout("o_tok", [ninst, NJ, 1024, 64], BF16)

    msk = k.sb("msk", [64, 256], BF16)
    msk2 = k.sb("msk2", [64, 320], BF16)
    arh = [[k.sb(f"arh{i}_{p}", [64, 8, 128], BF16) for p in range(2)]
           for i in range(ninst)]
    bhs = [[k.sb(f"bhs{i}_{p}", [64, 512], BF16) for p in range(2)]
           for i in range(ninst)]
    khs = [[k.sb(f"khs{i}_{p}", [64, 512], BF16) for p in range(2)]
           for i in range(ninst)]
    vts = [[k.sb(f"vts{i}_{p}", [64, 8, 64], BF16) for p in range(2)]
           for i in range(ninst)]
    kts = [[k.sb(f"kts{i}_{p}", [64, 8, 64], BF16) for p in range(2)]
           for i in range(ninst)]
    bts = [[k.sb(f"bts{i}_{p}", [64, 8, 64], BF16) for p in range(2)]
           for i in range(ninst)]
    eqs = [[k.sb(f"eqs{i}_{p}", [64, 8]) for p in range(2)] for i in range(ninst)]
    MS = [k.sb(f"MS{i}", [64, 5, 64], BF16) for i in range(ninst)]
    PQ = [[k.sb(f"PQ{i}_{p}", [64, 128], BF16) for p in range(2)]
          for i in range(ninst)]
    UC = [[k.sb(f"UC{i}_{p}", [64, 64], BF16) for p in range(2)]
          for i in range(ninst)]
    SS = [[k.sb(f"SSf{i}_{p}", [64, 64]) for p in range(2)] for i in range(ninst)]
    SSb = [k.sb(f"SSb{i}", [64, 64], BF16) for i in range(ninst)]
    o_q = [k.sb(f"oq{i}", [64, 512], BF16) for i in range(ninst)]
    PB = k.ps("PB", (128, 4096))

    s_c = k.sem()
    k.op("sync", "dma_start", out=msk[:], in_=MASKS, inc16=s_c)
    k.op("sync", "dma_start", out=msk2[:], in_=MASKS2, inc16=s_c)
    for i in range(ninst):
        k.op("vector", "memset", SS[1][i][:] if False else SS[i][1][:], 0.0)
        k.op("vector", "memset", SSb[i][:], 0.0)
    k.op("vector", "drain", inc=s_c)
    k.fence([s_c])

    s_i = [k.sem() for i in range(ninst)]
    s_ld = k.sem()
    s_st = k.sem()
    inst_done = [0] * ninst
    store_done = 0

    def emit_loads(g, pbuf):
        j, hh = g // 2, g % 2
        tsl = slice(hh * 512, (hh + 1) * 512)
        csl = slice(hh * 8, (hh + 1) * 8)
        for i in range(ninst):
            q = "sync" if i % 2 == 0 else "scalar"
            if g >= 2:
                k.wait(q, s_i[i], inst_done[i])
            k.op(q, "dma_start", out=arh[i][pbuf][:, :, 0:64],
                 in_=ah[i, j, :, tsl].rearrange("w (c t) -> w c t", t=64), inc16=s_ld)
            k.op(q, "dma_start", out=arh[i][pbuf][:, :, 64:128],
                 in_=rh[i, j, :, tsl].rearrange("w (c t) -> w c t", t=64), inc16=s_ld)
            k.op(q, "dma_start", out=bhs[i][pbuf][:], in_=bh[i, j, :, tsl], inc16=s_ld)
            k.op(q, "dma_start", out=khs[i][pbuf][:], in_=kh[i, j, :, tsl], inc16=s_ld)
            k.op(q, "dma_start", out=vts[i][pbuf][:],
                 in_=vt[i, j, tsl, :].rearrange("(c p) d -> p c d", p=64), inc16=s_ld)
            k.op(q, "dma_start", out=kts[i][pbuf][:],
                 in_=ktl[i, j, tsl, :].rearrange("(c p) d -> p c d", p=64), inc16=s_ld)
            k.op(q, "dma_start", out=bts[i][pbuf][:],
                 in_=btl[i, j, tsl, :].rearrange("(c p) d -> p c d", p=64), inc16=s_ld)
            k.op(q, "dma_start", out=eqs[i][pbuf][:], in_=eqc[i, j, :, csl], inc16=s_ld)

    emit_loads(0, 0)
    for g in range(NG):
        pbuf = g % 2
        ld_gate = s_ld.n
        if g + 1 < NG:
            emit_loads(g + 1, (g + 1) % 2)
        for cl in range(8):
            c = g * 8 + cl
            bank = lambda i: PB[:, 512 * i:512 * (i + 1)]
            ahc = lambda i: arh[i][pbuf][:, cl, 0:64]
            rhc = lambda i: arh[i][pbuf][:, cl, 64:128]
            bhc = lambda i: bhs[i][pbuf][:, cl * 64:(cl + 1) * 64]
            khc = lambda i: khs[i][pbuf][:, cl * 64:(cl + 1) * 64]
            vtc = lambda i: vts[i][pbuf][:, cl, :]
            for i in range(ninst):
                b = bank(i)
                if cl == 0:
                    k.wait("tensor", s_ld, ld_gate)
                k.wait("tensor", s_i[i], s_i[i].n)
                k.op("tensor", "matmul", b[0:64, 0:128], bhc(i), arh[i][pbuf][:, cl, :],
                     start=True, stop=True, skip_group_check=True)
                k.op("tensor", "matmul", b[0:64, 128:256], khc(i), arh[i][pbuf][:, cl, :],
                     start=False, stop=True, skip_group_check=True)
                k.op("tensor", "matmul", b[0:64, 256:320], ahc(i), bhc(i),
                     start=False, stop=True, skip_group_check=True, inc=s_i[i])
            for i in range(ninst):
                k.wait("vector", s_i[i], s_i[i].n)
                k.op("vector", "tensor_tensor", MS[i][:].rearrange("w a t -> w (a t)"),
                     bank(i)[0:64, 0:320], msk2[:], ALU.mult, inc=s_i[i])
            for i in range(ninst):
                b = bank(i)
                k.wait("tensor", s_i[i], s_i[i].n)
                k.op("tensor", "matmul", b[0:64, 0:64], ahc(i), SSb[i][:],
                     start=True, stop=False, skip_group_check=True)
                k.op("tensor", "matmul", b[0:64, 0:64], MS[i][:, 2, :], vtc(i),
                     start=False, stop=True, skip_group_check=True, inc=s_i[i])
            for i in range(ninst):
                k.wait("scalar", s_i[i], s_i[i].n)
                k.op("scalar", "activation", UC[i][0][:], bank(i)[0:64, 0:64],
                     AF.Identity, inc=s_i[i])
            k.op("scalar", "drain")
            for lv in range(6):
                for i in range(ninst):
                    b = bank(i)
                    uc = UC[i][lv % 2]
                    P_l = MS[i][:, 4, :] if lv == 0 else PQ[i][(lv + 1) % 2][:, 0:64]
                    Q_l = MS[i][:, 0, :] if lv == 0 else PQ[i][(lv + 1) % 2][:, 64:128]
                    k.wait("tensor", s_i[i], s_i[i].n)
                    k.op("tensor", "matmul", b[0:64, 64:128], Q_l, uc[:],
                         start=True, stop=False, skip_group_check=True)
                    k.op("tensor", "matmul", b[0:64, 64:128], msk[:, 192:256], uc[:],
                         start=False, stop=True, skip_group_check=True,
                         inc=(s_i[i] if lv == 5 else None))
                    if lv < 5:
                        k.op("tensor", "matmul", b[0:64, 128:192], Q_l, P_l,
                             start=False, stop=True, skip_group_check=True)
                        k.op("tensor", "matmul", b[0:64, 192:256], P_l, Q_l,
                             start=False, stop=True, skip_group_check=True,
                             inc=s_i[i])
                for i in range(ninst):
                    eng = "vector" if i % 2 == 0 else "scalar"
                    k.wait(eng, s_i[i], s_i[i].n)
                    if eng == "vector":
                        k.op(eng, "tensor_copy", UC[i][(lv + 1) % 2][:],
                             bank(i)[0:64, 64:128])
                        if lv < 5:
                            k.op(eng, "tensor_copy", PQ[i][lv % 2][:],
                                 bank(i)[0:64, 128:256])
                        k.op(eng, "drain", inc=s_i[i])
                    else:
                        k.op(eng, "activation", UC[i][(lv + 1) % 2][:],
                             bank(i)[0:64, 64:128], AF.Identity)
                        if lv < 5:
                            k.op(eng, "activation", PQ[i][lv % 2][:],
                                 bank(i)[0:64, 128:256], AF.Identity)
                        k.op(eng, "drain", inc=s_i[i])
            for i in range(ninst):
                b = bank(i)
                ufin = UC[i][0]
                k.wait("tensor", s_i[i], s_i[i].n)
                k.op("tensor", "matmul", b[0:64, 256:320], rhc(i), SSb[i][:],
                     start=True, stop=False, skip_group_check=True)
                k.op("tensor", "matmul", b[0:64, 256:320], MS[i][:, 1, :], ufin[:],
                     start=False, stop=False, skip_group_check=True)
                k.op("tensor", "matmul", b[0:64, 256:320], MS[i][:, 3, :], vtc(i),
                     start=False, stop=True, skip_group_check=True)
                k.op("tensor", "matmul", b[0:64, 320:384], bts[i][pbuf][:, cl, :],
                     ufin[:], start=False, stop=False, skip_group_check=True)
                k.op("tensor", "matmul", b[0:64, 320:384], kts[i][pbuf][:, cl, :],
                     vtc(i), start=False, stop=True, skip_group_check=True,
                     inc=s_i[i])
            for i in range(ninst):
                k.wait("scalar", s_i[i], s_i[i].n)
                if cl == 0 and g >= 1:
                    k.wait("scalar", s_st, store_done)
                k.op("scalar", "activation", o_q[i][:, cl * 64:(cl + 1) * 64],
                     bank(i)[0:64, 256:320], AF.Identity, inc=s_i[i])
            k.op("scalar", "drain")
            for i in range(ninst):
                k.wait("vector", s_i[i], s_i[i].n)
                sold = SS[i][(c + 1) % 2]
                snew = SS[i][c % 2]
                k.op("vector", "scalar_tensor_tensor", snew[:], sold[:],
                     eqs[i][pbuf][:, cl:cl + 1], bank(i)[0:64, 320:384],
                     ALU.mult, ALU.add)
                k.op("vector", "drain")
                k.op("vector", "tensor_copy", SSb[i][:], snew[:])
                k.op("vector", "drain", inc=s_i[i])
        j, hh = g // 2, g % 2
        for i in range(ninst):
            q = "sync" if i % 2 == 0 else "scalar"
            k.wait(q, s_i[i], s_i[i].n)
            k.op(q, "dma_start",
                 out=o_tok[i, j, hh * 512:(hh + 1) * 512, :]
                 .rearrange("(c p) d -> p c d", p=64),
                 in_=o_q[i][:].rearrange("p (c t) -> p c t", t=64), inc16=s_st)
            inst_done[i] = s_i[i].n
        store_done = s_st.n
    k.fence([s_st] + s_i)
    return k
